# version 22
# baseline (speedup 1.0000x reference)
import contextlib
import numpy as np
import concourse.bass as bass
import concourse.mybir as mybir
from concourse.bass_utils import run_bass_kernel_spmd

F32 = mybir.dt.float32
BF16 = mybir.dt.bfloat16
I32 = mybir.dt.int32
U8 = mybir.dt.uint8
AF = mybir.ActivationFunctionType
ALU = mybir.AluOpType
AX = mybir.AxisListType

P = 128
S = 2048
E = 2048
NH = 16
D = 128
C = 512
NCH = S // C
HID = 5632
NKE = E // P
NST = S // P
NHT = HID // P
EPS = 1e-5
SCALE = float(np.float64(1.0) / np.float64(np.sqrt(np.float64(D))))
MASKVAL = -3.0e9

DEBUG_SCRATCH = False
DEBUG_DUMPS = False
BISECT_IMMEDIATE_FLUSH = False


class _Eng:
    def __init__(self, name):
        self.name = name
        self.thunks = []
        self.n_c = 0
        self.waited = {}
        self.sem_c = None


class Res:
    __slots__ = ("name", "w", "r")

    def __init__(self, name):
        self.name = name
        self.w = None
        self.r = {}


class TR:

    def __init__(self):
        self.e = {n: _Eng(n) for n in ("PE", "DV", "AC", "GP", "SY")}
        self.streams = {}

    def stream(self, name):
        if name not in self.streams:
            self.streams[name] = [None, 0]
        return self.streams[name]

    def emit(self, eng, fn, reads=(), writes=(), dma=None):
        st = self.e[eng]
        waits = []

        def need(dep):
            if dep is None:
                return
            semkey, val = dep
            if semkey[0] == "D":
                val = self.streams[semkey[1]][1]
            got = st.waited.get(semkey, 0)
            if got >= val:
                return
            st.waited[semkey] = val
            waits.append((semkey, val))

        for r in reads:
            need(r.w)
        for w in writes:
            need(w.w)
            for dep in w.r.values():
                need(dep)

        if dma is not None:
            key = f"{eng}:{dma}"
            sd = self.stream(key)
            sd[1] += 16
            dep = (("D", key), sd[1])
            inc_amt = 16
        else:
            st.n_c += 1
            dep = (("C", eng), st.n_c)
            inc_amt = 1

        def thunk(engobj, _waits=tuple(waits), _fn=fn, _dep=dep,
                  _inc=inc_amt, _tr=self):
            for (semkey, val) in _waits:
                engobj.wait_ge(_tr._sem(semkey), val)
            last = _fn(engobj)
            last.then_inc(_tr._sem(_dep[0]), _inc)

        st.thunks.append(thunk)

        for r in reads:
            r.r[eng] = dep
        for w in writes:
            w.w = dep
            w.r = {}
        return dep

    def _sem(self, semkey):
        if semkey[0] == "D":
            return self.streams[semkey[1]][0]
        return self.e[semkey[1]].sem_c

    def raw_wait(self, eng, dep):
        st = self.e[eng]
        semkey, val = dep

        def thunk(engobj, _semkey=semkey, _val=val, _tr=self):
            engobj.wait_ge(_tr._sem(_semkey), _val)

        st.thunks.append(thunk)


def build_program():
    nc = bass.Bass("TRN2", target_bir_lowering=False, debug=False)

    x0_d = nc.dram_tensor("x0", [S, E], F32, kind="ExternalInput").ap()
    wqkvT_d = nc.dram_tensor("wqkvT", [E, 3 * E], F32, kind="ExternalInput").ap()
    woutT_d = nc.dram_tensor("woutT", [E, E], F32, kind="ExternalInput").ap()
    wgT_d = nc.dram_tensor("wgT", [E, HID], U8, kind="ExternalInput").ap()
    wuT_d = nc.dram_tensor("wuT", [E, HID], U8, kind="ExternalInput").ap()
    wdT_d = nc.dram_tensor("wdT", [HID, E], U8, kind="ExternalInput").ap()
    sgv_d = nc.dram_tensor("sgv", [P, NHT], F32, kind="ExternalInput").ap()
    suv_d = nc.dram_tensor("suv", [P, NHT], F32, kind="ExternalInput").ap()
    sd_d = nc.dram_tensor("sd", [1, E], F32, kind="ExternalInput").ap()
    maskq_d = nc.dram_tensor("maskq", [C, 2 * C], F32, kind="ExternalInput").ap()
    ident_d = nc.dram_tensor("ident", [P, P], F32, kind="ExternalInput").ap()
    out_d = nc.dram_tensor("out", [S, E], F32, kind="ExternalOutput").ap()

    scratch_kind = "ExternalOutput" if DEBUG_SCRATCH else "Internal"
    x1_d = nc.dram_tensor("x1s", [S, E], F32, kind=scratch_kind).ap()
    atT_d = nc.dram_tensor("atTs", [NST, P, NH, P], BF16, kind=scratch_kind).ap()
    zT_d = nc.dram_tensor("zTs", [NST, P, NHT, P], BF16, kind=scratch_kind).ap()
    y_d = nc.dram_tensor("ys", [S, E], F32, kind=scratch_kind).ap()
    rec_d = nc.dram_tensor("recs", [NST, E], F32, kind="Internal").ap()
    if DEBUG_SCRATCH:
        wdbg_d = nc.dram_tensor("wdbg", [2, NHT, P, 1024], BF16,
                                kind="ExternalOutput").ap()
        zdbg_d = nc.dram_tensor("zdbg", [2, P, NHT * P], BF16,
                                kind="ExternalOutput").ap()

    tr = TR()
    ctx = contextlib.ExitStack()

    with ctx:
        E_ = ctx.enter_context
        AT = E_(nc.sbuf_tensor("AT", [P, 16 * 2048], BF16))
        BB = E_(nc.sbuf_tensor("BB", [P, 16 * 2048], BF16))
        SM = E_(nc.sbuf_tensor("SM", [P, 8 * 2048], BF16))
        CF = E_(nc.sbuf_tensor("CF", [P, 4 * 2048], F32))
        MK = E_(nc.sbuf_tensor("MK", [P, 4 * 1024], BF16))
        IDB = E_(nc.sbuf_tensor("IDB", [P, P], BF16))
        IDF = E_(nc.sbuf_tensor("IDF", [P, P], F32))
        AB = E_(nc.sbuf_tensor("AB", [P, 1024], BF16))
        STT = E_(nc.sbuf_tensor("STT", [P, 256], F32))
        DEN = E_(nc.sbuf_tensor("DEN", [P, 256], F32))
        DENR = E_(nc.sbuf_tensor("DENR", [P, 256], F32))
        RCS = E_(nc.sbuf_tensor("RCS", [P, P], F32))

        ps0 = E_(nc.psum_tensor("ps0", [P, 2048], F32))
        ps1 = E_(nc.psum_tensor("ps1", [P, 2048], F32))
        ps1b = ps1.bitcast(BF16)

        bank = [Res(f"bank{i}") for i in range(8)]
        rAT = Res("AT")
        bbp = [Res(f"BB{i}") for i in range(16)]
        smp = [Res(f"SM{i}") for i in range(8)]
        cfs = [Res(f"CF{i}") for i in range(4)]
        rMK = Res("MK")
        rIDB = Res("IDB")
        rIDF = Res("IDF")
        rABs = [Res("AB0"), Res("AB1")]
        stc = {}

        def st_res(name):
            if name not in stc:
                stc[name] = Res(name)
            return stc[name]

        rx1 = [Res(f"x1_{i}") for i in range(NST)]
        rDEN = Res("DEN")
        rDENR = Res("DENR")
        rRCS = Res("RCS")
        rREC = [Res(f"RECscr{i}") for i in range(NST)]
        ratT = Res("atT")
        rzT = Res("zT")
        ry = [[Res(f"y_{i}_{h}") for h in range(2)] for i in range(NST)]

        _stcol = [0]

        def stcol(n=1):
            c = _stcol[0]
            _stcol[0] += n
            assert _stcol[0] <= 256
            return STT[:, c:c + n]

        sgv_t = stcol(NHT)
        suv_t = stcol(NHT)
        eps_t = stcol(1)

        def BBpg(i, n=1):
            return BB[:, i * 2048:(i + n) * 2048]

        def SMpg(i, n=1):
            return SM[:, i * 2048:(i + n) * 2048]

        SMf = SM.bitcast(F32)

        def SMf_pg(i, n=1):
            return SMf[:, i * 1024:(i + n) * 1024]

        def CFs(i, n=1):
            return CF[:, i * 2048:(i + n) * 2048]

        CFi = CF.bitcast(I32)

        def CFi_s(i, n=1):
            return CFi[:, i * 2048:(i + n) * 2048]

        CFu = CF.bitcast(U8)

        def CFu_s(i):
            return CFu[:, i * 8192:i * 8192 + 2048]

        def dma(eng, dst, src, stream, reads=(), writes=()):
            return tr.emit(eng, lambda e, _d=dst, _s=src: e.dma_start(_d, _s),
                           reads=reads, writes=writes, dma=stream)

        def ln_stats(pref, src_ap, width, src_reads, junk_ap, junk_res,
                     extra_sum=None):
            s1 = st_res(pref + "s1"); s2 = st_res(pref + "s2")
            m = st_res(pref + "m"); mq = st_res(pref + "mq")
            va = st_res(pref + "va"); lv = st_res(pref + "lv")
            rs = st_res(pref + "rs"); nm = st_res(pref + "nm")
            a_s1 = stcol_of(pref + "s1"); a_s2 = stcol_of(pref + "s2")
            a_m = stcol_of(pref + "m"); a_mq = stcol_of(pref + "mq")
            a_va = stcol_of(pref + "va"); a_lv = stcol_of(pref + "lv")
            a_rs = stcol_of(pref + "rs"); a_nm = stcol_of(pref + "nm")

            tr.emit("DV", lambda e: e.reduce_sum(a_s1, src_ap, axis=AX.X),
                    reads=src_reads, writes=[s1])
            tr.emit("DV", lambda e: e.scalar_tensor_tensor(
                junk_ap, src_ap, 1.0, src_ap, ALU.mult, ALU.mult,
                accum_out=a_s2), reads=src_reads, writes=[s2, junk_res])
            tr.emit("DV", lambda e: e.tensor_scalar(
                a_m, a_s1, 1.0 / width, None, ALU.mult),
                reads=[s1], writes=[m])
            tr.emit("DV", lambda e: e.scalar_tensor_tensor(
                a_mq, a_m, -1.0, a_m, ALU.mult, ALU.mult),
                reads=[m], writes=[mq])
            tr.emit("DV", lambda e: e.scalar_tensor_tensor(
                a_va, a_s2, 1.0 / width, a_mq, ALU.mult, ALU.add),
                reads=[s2, mq], writes=[va])
            tr.emit("AC", lambda e: e.activation(
                a_lv, a_va, AF.Ln, bias=eps_t, scale=1.0),
                reads=[va, st_res("eps")], writes=[lv])
            tr.emit("AC", lambda e: e.activation(
                a_rs, a_lv, AF.Exp, bias=0.0, scale=-0.5),
                reads=[lv], writes=[rs])
            tr.emit("DV", lambda e: e.scalar_tensor_tensor(
                a_nm, a_m, -1.0, a_rs, ALU.mult, ALU.mult),
                reads=[m, rs], writes=[nm])
            return a_rs, a_nm, rs, nm

        _st_aps = {}

        def stcol_of(name, n=1):
            if name not in _st_aps:
                _st_aps[name] = stcol(n)
            return _st_aps[name]

        tr.emit("DV", lambda e: e.memset(eps_t, EPS), writes=[st_res("eps")])
        dma("SY", sgv_t, sgv_d, "sg", writes=[st_res("sgv")])
        dma("SY", suv_t, suv_d, "su", writes=[st_res("suv")])
        dma("SY", CFs(0)[:, 0:P], ident_d, "cf0", writes=[cfs[0]])
        tr.emit("DV", lambda e: e.tensor_copy(IDB[:], CFs(0)[:, 0:P]),
                reads=[cfs[0]], writes=[rIDB])
        tr.emit("DV", lambda e: e.tensor_copy(IDF[:], CFs(0)[:, 0:P]),
                reads=[cfs[0]], writes=[rIDF])
        dma("SY", CFs(2, 2).rearrange("p (t c) -> p t c", c=1024),
            maskq_d.rearrange("(t p) c -> p t c", p=P),
            "cf2", writes=[cfs[2], cfs[3]])
        tr.emit("DV", lambda e: e.tensor_copy(MK[:], CFs(2, 2)),
                reads=[cfs[2], cfs[3]], writes=[rMK])

        def mask_tile(t):
            return MK[:, t * 1024:(t + 1) * 1024]

        for st in range(NST):
            sl = st % 2
            dma("GP", CFs(sl), x0_d[st * P:(st + 1) * P, :], f"cf{sl}",
                writes=[cfs[sl]])
            rstd, negmr, r_rs, r_nm = ln_stats(
                "A0", CFs(sl), E, [cfs[sl]], BBpg(0), bbp[0])
            hp = st % 2
            tr.emit("DV", lambda e, _sl=sl, _hp=hp, _rs=rstd, _nm=negmr:
                    e.tensor_scalar(SMpg(_hp), CFs(_sl), _rs, _nm,
                                    ALU.mult, ALU.add),
                    reads=[cfs[sl], r_rs, r_nm], writes=[smp[hp]])
            boff = 0 if st % 2 == 0 else 2048
            bks = [bank[4], bank[5]] if st % 2 == 0 else [bank[6], bank[7]]

            def tp_fn(e, _hp=hp, _boff=boff):
                for k in range(NKE):
                    ins = e.transpose(
                        ps1b[:, _boff + k * P:_boff + (k + 1) * P],
                        SMpg(_hp)[:, k * P:(k + 1) * P], IDB[:])
                return ins
            tr.emit("PE", tp_fn, reads=[smp[hp], rIDB], writes=bks)

            def ev_fn(e, _st=st, _boff=boff):
                dst = AT[:].rearrange("p (k c) -> p k c", c=2048)[
                    :, :, _st * P:(_st + 1) * P]
                src = ps1b[:, _boff:_boff + 2048].rearrange(
                    "p (k c) -> p k c", c=P)
                return e.copy(dst, src)
            tr.emit("AC", ev_fn, reads=bks, writes=[rAT])

        def h1T(k, lo, n):
            return AT[:, k * 2048 + lo:k * 2048 + lo + n]

        pe_fill = []

        def drain_fill(n):
            for _ in range(min(n, len(pe_fill))):
                pe_fill.pop(0)()

        def stage_v_group(g):
            dma("SY", CF[:].rearrange("p (k c) -> p k c", c=512),
                wqkvT_d.rearrange("(k p) f -> p k f", p=P)[
                    :, :, 2 * E + g * 512:2 * E + (g + 1) * 512],
                "cf0", writes=cfs)
            tr.emit("DV", lambda e: e.tensor_copy(BBpg(4, 4), CF[:]),
                    reads=cfs, writes=[bbp[4], bbp[5], bbp[6], bbp[7]])

        def enq_v_mms():
            for q4 in range(4):
                def unit(_q4=q4):
                    for vst in range(_q4 * 4, _q4 * 4 + 4):
                        vb = bank[4 + vst % 2]
                        pso = ps1[:, (vst % 2) * 512:(vst % 2) * 512 + 512]

                        def vmm(e, _vst=vst, _pso=pso):
                            for k in range(NKE):
                                ins = e.matmul(
                                    _pso,
                                    AT[:, k * 2048 + _vst * P:
                                       k * 2048 + (_vst + 1) * P],
                                    BB[:, 4 * 2048 + k * 512:
                                       4 * 2048 + (k + 1) * 512],
                                    start=(k == 0), stop=(k == NKE - 1))
                            return ins
                        tr.emit("PE", vmm,
                                reads=[rAT, bbp[4], bbp[5], bbp[6], bbp[7]],
                                writes=[vb])
                        tr.emit("AC", lambda e, _vst=vst, _pso=pso: e.copy(
                            BB[:, (_vst // 4) * 2048 + (_vst % 4) * 512:
                               (_vst // 4) * 2048 + (_vst % 4) * 512 + 512],
                            _pso), reads=[vb], writes=[bbp[vst // 4]])
                pe_fill.append(unit)

        def stage_qk(hh):
            par = hh % 2
            for (slot, fcol, wpg, wap) in (
                    (par * 2, hh * P, bbp[8 + 2 * par], BBpg(8 + 2 * par)),
                    (par * 2 + 1, E + hh * P, bbp[9 + 2 * par],
                     BBpg(9 + 2 * par))):
                stg = cfs[slot]
                stg_ap = CFs(slot)
                dma("SY", stg_ap.rearrange("p (k c) -> p k c", c=P),
                    wqkvT_d.rearrange("(k p) f -> p k f", p=P)[
                        :, :, fcol:fcol + P],
                    f"cf{slot}", writes=[stg])
                tr.emit("DV", lambda e, _w=wap, _s=stg_ap:
                        e.tensor_copy(_w, _s), reads=[stg], writes=[wpg])

        def enq_qk_mms(hh):
            par = hh % 2
            for (isq, wpg, wap, opg, oap) in (
                    (True, bbp[8 + 2 * par], BBpg(8 + 2 * par),
                     bbp[12 + 2 * par], BBpg(12 + 2 * par)),
                    (False, bbp[9 + 2 * par], BBpg(9 + 2 * par),
                     bbp[13 + 2 * par], BBpg(13 + 2 * par))):
                for half in range(2):
                    for n in range(2):
                        def unit(_wpg=wpg, _wap=wap, _opg=opg, _oap=oap,
                                 _half=half, _n=n, _isq=isq):
                            def qkmm(e, _w=_wap, _h2=_half, _nn=_n):
                                for k in range(NKE):
                                    ins = e.matmul(
                                        ps1[:, _nn * 512:(_nn + 1) * 512],
                                        _w[:, k * P:(k + 1) * P],
                                        h1T(k, _h2 * 1024 + _nn * 512, 512),
                                        start=(k == 0), stop=(k == NKE - 1))
                                return ins
                            tr.emit("PE", qkmm, reads=[_wpg, rAT],
                                    writes=[bank[4 + _n]])
                            if _n == 1:
                                sc = SCALE if _isq else 1.0
                                tr.emit("AC", lambda e, _o=_oap, _h2=_half,
                                        _sc=sc:
                                        e.activation(
                                            _o[:, _h2 * 1024:
                                               (_h2 + 1) * 1024],
                                            ps1[:, 0:1024], AF.Copy,
                                            bias=0.0, scale=_sc),
                                        reads=[bank[4], bank[5]],
                                        writes=[_opg])
                        pe_fill.append(unit)

        stage_v_group(0)
        stage_qk(0)
        enq_v_mms()
        enq_qk_mms(0)
        drain_fill(99)

        for h in range(NH):
            par = h % 2
            if h + 1 < NH:
                stage_qk(h + 1)
                enq_qk_mms(h + 1)

            def vtile(sk_t, hh):
                return BB[:, (sk_t // 4) * 2048 + (sk_t % 4) * 512
                          + (hh % 4) * P:
                          (sk_t // 4) * 2048 + (sk_t % 4) * 512
                          + (hh % 4) * P + P]

            qT_pg, kT_pg = bbp[12 + 2 * par], bbp[13 + 2 * par]
            pend_tp = [None]

            def flush_tp():
                if pend_tp[0] is not None:
                    pend_tp[0]()
                    pend_tp[0] = None

            for chn in range(NCH):
                nsk = C if chn == 0 else 2 * C
                ksk0 = (chn - 1) * C if chn > 0 else 0
                nskt = nsk // P
                cpar = chn % 2
                probs_pg = [smp[2 * cpar], smp[2 * cpar + 1]]
                probs_ap = SMpg(2 * cpar, 2)
                probsT_pg = [smp[4 + 2 * cpar], smp[5 + 2 * cpar]]
                probsT_ap = SMpg(4 + 2 * cpar, 2)

                for t in range(4):
                    sq0 = chn * C + t * P
                    reg = (t % 2) * 1024
                    sc_bks = [bank[2 * (t % 2)], bank[2 * (t % 2) + 1]]

                    def smm(e, _t=t, _sq0=sq0, _chn=chn, _par=par,
                            _nsk=nsk, _ksk0=ksk0, _reg=reg):
                        scp = ps0[:, _reg:_reg + _nsk]
                        mkap = (mask_tile(_t)[:, 512:1024] if _chn == 0
                                else mask_tile(_t))
                        for half in range(_nsk // 512):
                            e.matmul(
                                scp[:, half * 512:(half + 1) * 512],
                                BB[:, (12 + 2 * _par) * 2048 + _sq0:
                                   (12 + 2 * _par) * 2048 + _sq0 + P],
                                BB[:, (13 + 2 * _par) * 2048 + _ksk0
                                   + half * 512:
                                   (13 + 2 * _par) * 2048 + _ksk0
                                   + half * 512 + 512],
                                start=True, stop=False)
                            ins = e.matmul(
                                scp[:, half * 512:(half + 1) * 512],
                                IDB[:],
                                mkap[:, half * 512:(half + 1) * 512],
                                start=False, stop=True)
                        return ins
                    tr.emit("PE", smm,
                            reads=[qT_pg, kT_pg, rMK, rIDB], writes=sc_bks)

                    nbr = st_res(f"nb{par}_{cpar}_{t}")
                    a_nb = stcol_of(f"nb{par}_{cpar}_{t}")
                    den_ap = DEN[:, h * 16 + chn * 4 + t:
                                 h * 16 + chn * 4 + t + 1]

                    tr.emit("DV", lambda e, _nsk=nsk, _a=a_nb, _reg=reg:
                            e.reduce_max(_a, ps0[:, _reg:_reg + _nsk],
                                         axis=AX.X, negate=True),
                            reads=sc_bks, writes=[nbr])
                    pr_ap = probs_ap[:, t * nsk:t * nsk + nsk]
                    pgi = (t * nsk) // 2048
                    tr.emit("AC", lambda e, _nsk=nsk, _pr=pr_ap, _reg=reg,
                            _nb=a_nb, _dn=den_ap:
                            e.activation(_pr, ps0[:, _reg:_reg + _nsk],
                                         AF.Exp, bias=_nb, scale=1.0,
                                         accum_out=_dn),
                            reads=sc_bks + [nbr],
                            writes=[rDEN, probs_pg[pgi]])
                    if t == 1:
                        drain_fill(1)

                flush_tp()
                drain_fill(1)

                def mk_tp(_chn=chn, _nsk=nsk, _nskt=nskt, _par=par,
                          _h=h, _probs_ap=probs_ap,
                          _probs_pg=tuple(probs_pg),
                          _probsT_ap=probsT_ap,
                          _probsT_pg=tuple(probsT_pg)):
                    def run():
                        npair = _nskt // 2
                        pv_ps = ps1[:, 1536:2048]
                        pv_bk = bank[7]
                        for pair in range(npair):
                            ptb = 6 + pair % 2
                            pt_ps = ps1b[:, (ptb - 4) * 1024:
                                         (ptb - 3) * 1024]
                            pt_bk = bank[ptb]

                            def tpf(e, _pair=pair, _pt=pt_ps):
                                for t in range(4):
                                    for j in range(2):
                                        skt = _pair * 2 + j
                                        ins = e.transpose(
                                            _pt[:, j * 512 + t * P:
                                                j * 512 + (t + 1) * P],
                                            _probs_ap[:, t * _nsk + skt * P:
                                                      t * _nsk
                                                      + (skt + 1) * P],
                                            IDB[:])
                                return ins
                            tr.emit("PE", tpf,
                                    reads=[_probs_pg[0], _probs_pg[1],
                                           rIDB],
                                    writes=[pt_bk])
                            tr.emit("DV", lambda e, _pair=pair, _pt=pt_ps:
                                    e.tensor_copy(
                                        _probsT_ap[:, _pair * 1024:
                                                   (_pair + 1) * 1024],
                                        _pt),
                                    reads=[pt_bk],
                                    writes=[_probsT_pg[pair // 2]])

                        def pvf(e):
                            for skt in range(_nskt):
                                glob_skt = ((0 if _chn == 0
                                             else (_chn - 1) * 4) + skt)
                                ins = e.matmul(
                                    pv_ps, vtile(glob_skt, _h),
                                    _probsT_ap[:, (skt // 2) * 1024
                                               + (skt % 2) * 512:
                                               (skt // 2) * 1024
                                               + (skt % 2) * 512 + 512],
                                    start=(skt == 0),
                                    stop=(skt == _nskt - 1))
                            return ins
                        tr.emit("PE", pvf,
                                reads=[_probsT_pg[0], _probsT_pg[1],
                                       bbp[0], bbp[1], bbp[2], bbp[3]],
                                writes=[pv_bk])
                        ab = rABs[_chn % 2]
                        ab_ap = AB[:, (_chn % 2) * 512:
                                   (_chn % 2) * 512 + 512]
                        tr.emit("AC", lambda e, _ab=ab_ap:
                                e.copy(_ab, pv_ps), reads=[pv_bk],
                                writes=[ab])
                        dma("GP",
                            atT_d[_chn * 4:(_chn + 1) * 4, :, _h, :]
                            .transpose([1, 0, 2]),
                            ab_ap.rearrange("p (t c) -> p t c", c=P),
                            "atw", reads=[ab], writes=[ratT])
                    return run
                pend_tp[0] = mk_tp()
                if BISECT_IMMEDIATE_FLUSH:
                    flush_tp()

            flush_tp()
            drain_fill(1)

            if h + 1 < NH and (h + 1) % 4 == 0:
                stage_v_group((h + 1) // 4)
                enq_v_mms()
            drain_fill(2)
        drain_fill(99)

        for k in range(NKE):
            sl = 2 + (k % 2)
            dma("SY", CFs(sl), woutT_d[k * P:(k + 1) * P, :], f"cf{sl}",
                writes=[cfs[sl]])
            tr.emit("DV", lambda e, _k=k, _sl=sl:
                    e.tensor_copy(BBpg(_k), CFs(_sl)),
                    reads=[cfs[sl]], writes=[bbp[k]])

        y_sb = SMf_pg(0, 2)
        x1_sb = SMf_pg(2, 2)
        pend_tp2 = [None]

        tr.emit("DV", lambda e: e.reciprocal(DENR[:], DEN[:]),
                reads=[rDEN], writes=[rDENR])
        def rec_build(st2):
            base = (st2 // 4) * 4 + st2 % 4
            q = st2 % 4
            tp_ps = ps1[0:16, q * 128:q * 128 + 128]

            def rtp(e, _b=base, _p=tp_ps):
                return e.transpose(_p, DENR[:, _b::16], IDF[:])
            tr.emit("PE", rtp, reads=[rDENR, rIDF], writes=[bank[4]])
            tr.emit("AC", lambda e, _p=tp_ps: e.copy(RCS[0:16, :], _p),
                    reads=[bank[4]], writes=[rRCS])
            dma("GP", rec_d[st2].rearrange("(a c) -> a c", a=16),
                RCS[0:16, :], "recw", reads=[rRCS],
                writes=[rREC[st2]])

        def at_load(st2):
            asl2 = 4 + st2 % 2
            dma("AC", SMpg(asl2), atT_d[st2].rearrange("p h c -> p (h c)"),
                f"sm{asl2}", reads=[ratT], writes=[smp[asl2]])
            csl = 2 + st2 % 2
            dma("SY", CFs(csl),
                rec_d[st2].rearrange("(o c) -> o c", o=1)
                .partition_broadcast(P),
                f"cf{csl}", reads=[rREC[st2]], writes=[cfs[csl]])
            tr.emit("DV", lambda e, _a=SMpg(asl2), _c=CFs(csl):
                    e.tensor_tensor(_a, _a, _c, ALU.mult),
                    reads=[smp[asl2], cfs[csl]],
                    writes=[smp[asl2]])

        rec_build(0)
        rec_build(1)
        rec_build(2)
        at_load(0)
        at_load(1)
        for st in range(NST):
            asl = 4 + st % 2
            dma("GP", CFs(st % 2), x0_d[st * P:(st + 1) * P, :],
                f"cf{st % 2}", writes=[cfs[st % 2]])

            def projmm(e, _asl=asl):
                for n in range(4):
                    for k in range(NKE):
                        ins = e.matmul(
                            ps0[:, n * 512:(n + 1) * 512],
                            SM[:, _asl * 2048 + k * P:
                               _asl * 2048 + (k + 1) * P],
                            BB[:, k * 2048 + n * 512:k * 2048 + n * 512 + 512],
                            start=(k == 0), stop=(k == NKE - 1))
                return ins
            tr.emit("PE", projmm,
                    reads=[smp[asl]] + bbp[0:16], writes=bank[0:4])
            if st + 3 < NST:
                rec_build(st + 3)
            if st + 2 < NST:
                at_load(st + 2)
            if pend_tp2[0] is not None:
                pend_tp2[0]()
                pend_tp2[0] = None
            s1p = [st_res(f"A3p{i}") for i in range(4)]
            for n in range(4):
                a_p = stcol_of(f"A3p{n}")
                tr.emit("AC", lambda e, _n=n, _a=a_p: e.activation(
                    y_sb[:, _n * 512:(_n + 1) * 512],
                    ps0[:, _n * 512:(_n + 1) * 512],
                    AF.Copy, bias=0.0, scale=1.0, accum_out=_a),
                    reads=[bank[n]], writes=[s1p[n], smp[n // 2]])
            rs1 = st_res("A3s1")
            a_s1 = stcol_of("A3s1")
            tr.emit("DV", lambda e: e.tensor_tensor(
                a_s1, stcol_of("A3p0"), stcol_of("A3p1"), ALU.add),
                reads=[s1p[0], s1p[1]], writes=[rs1])
            tr.emit("DV", lambda e: e.tensor_tensor(
                a_s1, a_s1, stcol_of("A3p2"), ALU.add),
                reads=[rs1, s1p[2]], writes=[rs1])
            tr.emit("DV", lambda e: e.tensor_tensor(
                a_s1, a_s1, stcol_of("A3p3"), ALU.add),
                reads=[rs1, s1p[3]], writes=[rs1])
            rs2 = st_res("A3s2")
            a_s2 = stcol_of("A3s2")
            tr.emit("DV", lambda e: e.scalar_tensor_tensor(
                SMpg(7), y_sb, 1.0, y_sb, ALU.mult, ALU.mult,
                accum_out=a_s2),
                reads=[smp[0], smp[1]], writes=[rs2, smp[7]])
            a_m = stcol_of("A3m"); a_mq = stcol_of("A3mq")
            a_va = stcol_of("A3va"); a_lv = stcol_of("A3lv")
            a_rs = stcol_of("A3rs"); a_nm = stcol_of("A3nm")
            rm = st_res("A3m"); rmq = st_res("A3mq"); rva = st_res("A3va")
            rlv = st_res("A3lv"); rrs = st_res("A3rs"); rnm = st_res("A3nm")
            tr.emit("DV", lambda e: e.tensor_scalar(
                a_m, a_s1, 1.0 / E, None, ALU.mult), reads=[rs1], writes=[rm])
            tr.emit("DV", lambda e: e.scalar_tensor_tensor(
                a_mq, a_m, -1.0, a_m, ALU.mult, ALU.mult),
                reads=[rm], writes=[rmq])
            tr.emit("DV", lambda e: e.scalar_tensor_tensor(
                a_va, a_s2, 1.0 / E, a_mq, ALU.mult, ALU.add),
                reads=[rs2, rmq], writes=[rva])
            tr.emit("AC", lambda e: e.activation(
                a_lv, a_va, AF.Ln, bias=eps_t, scale=1.0),
                reads=[rva, st_res("eps")], writes=[rlv])
            tr.emit("AC", lambda e: e.activation(
                a_rs, a_lv, AF.Exp, bias=0.0, scale=-0.5),
                reads=[rlv], writes=[rrs])
            tr.emit("DV", lambda e: e.scalar_tensor_tensor(
                a_nm, a_m, -1.0, a_rs, ALU.mult, ALU.mult),
                reads=[rm, rrs], writes=[rnm])
            tr.emit("DV", lambda e: e.tensor_scalar(
                y_sb, y_sb, a_rs, a_nm, ALU.mult, ALU.add),
                reads=[rrs, rnm, smp[0], smp[1]], writes=[smp[0], smp[1]])
            tr.emit("DV", lambda e, _sl=st % 2: e.tensor_tensor(
                x1_sb, y_sb, CFs(_sl), ALU.add),
                reads=[smp[0], smp[1], cfs[st % 2]], writes=[smp[2], smp[3]])
            dma("GP", x1_d[st * P:(st + 1) * P, :], x1_sb,
                f"x1w{st}", reads=[smp[2], smp[3]], writes=[rx1[st]])
            rstd2, negmr2, r_rs2, r_nm2 = ln_stats(
                "A3b", x1_sb, E, [smp[2], smp[3]], SMpg(7), smp[7])
            tr.emit("DV", lambda e, _rs=rstd2, _nm=negmr2: e.tensor_scalar(
                SMpg(6), x1_sb, _rs, _nm, ALU.mult, ALU.add),
                reads=[smp[2], smp[3], r_rs2, r_nm2], writes=[smp[6]])
            boff = 0 if st % 2 == 0 else 2048
            bks = [bank[4], bank[5]] if st % 2 == 0 else [bank[6], bank[7]]

            def mk_pend(_st=st, _boff=boff, _bks=bks):
                def run():
                    def tp2(e):
                        for k in range(NKE):
                            ins = e.transpose(
                                ps1b[:, _boff + k * P:_boff + (k + 1) * P],
                                SMpg(6)[:, k * P:(k + 1) * P], IDB[:])
                        return ins
                    tr.emit("PE", tp2, reads=[smp[6], rIDB], writes=_bks)

                    def ev2(e):
                        dst = AT[:].rearrange("p (k c) -> p k c", c=2048)[
                            :, :, _st * P:(_st + 1) * P]
                        srcp = ps1b[:, _boff:_boff + 2048].rearrange(
                            "p (k c) -> p k c", c=P)
                        return e.copy(dst, srcp)
                    tr.emit("AC", ev2, reads=_bks, writes=[rAT])
                return run
            pend_tp2[0] = mk_pend()

        if pend_tp2[0] is not None:
            pend_tp2[0]()
            pend_tp2[0] = None

        rzz = [Res("zzA"), Res("zzB"), Res("zzC")]
        rwd = [Res(f"wd{i}") for i in range(8)]

        def wd_ap(kt):
            if kt < 32:
                return BB[:, kt * 1024:(kt + 1) * 1024]
            return SM[:, (kt - 32) * 1024:(kt - 31) * 1024]

        def wd_res(kt):
            return rwd[kt // 6]

        def f3_load_wd(nh, kts):
            for kt in kts:
                sl = kt % 4
                dma("SY" if kt % 2 == 0 else "AC", CFu_s(sl)[:, 0:1024],
                    wdT_d[kt * P:(kt + 1) * P,
                          nh * 1024:(nh + 1) * 1024],
                    f"cf{sl}", writes=[cfs[sl]])
                alias_pg = bbp[kt // 2] if kt < 32 else smp[(kt - 32) // 2]
                tr.emit("DV", lambda e, _kt=kt, _sl=sl: e.tensor_scalar(
                    wd_ap(_kt), CFu_s(_sl)[:, 0:1024], 128.0, None,
                    ALU.subtract),
                    reads=[cfs[sl]], writes=[wd_res(kt), alias_pg])

        NBLK = HID // 512

        def f2_load_block(b):
            bpar = b % 2
            for (wsrc, base, slb) in ((wgT_d, bpar * 8, 0),
                                      (wuT_d, bpar * 8 + 4, 2)):
                for q in range(4):
                    sl = slb + q % 2
                    dma("SY", CFu_s(sl).rearrange("p (k c) -> p k c", c=512),
                        wsrc.rearrange("(k p) f -> p k f", p=P)[
                            :, 4 * q:4 * q + 4, b * 512:(b + 1) * 512],
                        f"cf{sl}", writes=[cfs[sl]])
                    tr.emit("DV", lambda e, _q=q, _sl=sl, _base=base:
                            e.tensor_scalar(BBpg(_base + _q), CFu_s(_sl),
                                            128.0, None, ALU.subtract),
                            reads=[cfs[sl]], writes=[bbp[base + q]])

        f2_load_block(0)
        for b in range(NBLK):
            bpar = b % 2
            wg_pgs = bbp[bpar * 8: bpar * 8 + 4]
            wu_pgs = bbp[bpar * 8 + 4: bpar * 8 + 8]
            for htl in range(4):
                if htl == 1 and b + 1 < NBLK:
                    f2_load_block(b + 1)
                if htl == 1 and b == NBLK - 1:
                    f3_load_wd(0, range(16, 32))
                ht = b * 4 + htl
                hpar = htl % 2
                g_pg, u_pg, z_pg = smp[hpar * 3], smp[hpar * 3 + 1], \
                    smp[hpar * 3 + 2]
                g_ap, u_ap, z_ap = SMpg(hpar * 3), SMpg(hpar * 3 + 1), \
                    SMpg(hpar * 3 + 2)

                def gmm(e, _base=bpar * 8, _htl=htl):
                    for k in range(NKE):
                        for n in range(4):
                            ins = e.matmul(
                                ps0[:, n * 512:(n + 1) * 512],
                                BB[:, (_base + k // 4) * 2048
                                   + (k % 4) * 512 + _htl * P:
                                   (_base + k // 4) * 2048
                                   + (k % 4) * 512 + _htl * P + P],
                                h1T(k, n * 512, 512),
                                start=(k == 0), stop=(k == NKE - 1))
                    return ins
                tr.emit("PE", gmm, reads=wg_pgs + [rAT], writes=bank[0:4])
                tr.emit("AC", lambda e, _ht=ht, _g=g_ap: e.activation(
                    _g, ps0[:], AF.Silu, bias=0.0,
                    scale=sgv_t[:, _ht:_ht + 1]),
                    reads=bank[0:4] + [st_res("sgv")], writes=[g_pg])

                def umm(e, _base=bpar * 8 + 4, _htl=htl):
                    for k in range(NKE):
                        for n in range(4):
                            ins = e.matmul(
                                ps1[:, n * 512:(n + 1) * 512],
                                BB[:, (_base + k // 4) * 2048
                                   + (k % 4) * 512 + _htl * P:
                                   (_base + k // 4) * 2048
                                   + (k % 4) * 512 + _htl * P + P],
                                h1T(k, n * 512, 512),
                                start=(k == 0), stop=(k == NKE - 1))
                    return ins
                tr.emit("PE", umm, reads=wu_pgs + [rAT], writes=bank[4:8])
                tr.emit("AC", lambda e, _ht=ht, _u=u_ap: e.activation(
                    _u, ps1[:], AF.Copy, bias=0.0,
                    scale=suv_t[:, _ht:_ht + 1]),
                    reads=bank[4:8] + [st_res("suv")], writes=[u_pg])
                tr.emit("DV", lambda e, _g=g_ap, _u=u_ap, _z=z_ap:
                        e.tensor_tensor(_z, _g, _u, ALU.mult),
                        reads=[g_pg, u_pg], writes=[z_pg])
                dma("GP",
                    zT_d[:, :, ht, :].transpose([1, 0, 2]),
                    z_ap.rearrange("p (t c) -> p t c", c=P),
                    "zTw", reads=[z_pg], writes=[rzT])


        y_half = SMf[:, 6 * 1024:7 * 1024]
        sdb_half = SMf[:, 7 * 1024:8 * 1024]
        junk_f3 = AT[:, 18432:19456]
        rjunk_f3 = Res("junkF3")
        ATf = AT.bitcast(F32)
        y_tmp = ATf[:, 9728:10752]
        rytmp = Res("ytmpF3")

        outsd = []
        for nh in range(2):
            dma("SY", sdb_half,
                sd_d[:, nh * 1024:(nh + 1) * 1024].partition_broadcast(P),
                "sdb", writes=[smp[7]])
            kts = ([kt for kt in range(NHT) if not (16 <= kt < 32)]
                   if nh == 0 else range(NHT))
            f3_load_wd(nh, kts)
            if DEBUG_DUMPS:
                for kt in range(NHT):
                    dma("GP", wdbg_d[nh, kt], wd_ap(kt), "dbg",
                        reads=[wd_res(kt)], writes=[st_res(f"wdbg{nh}_{kt}")])
            for st in range(NST):
                zsl = st % 3
                zz_ap = AT[:, zsl * 6144: zsl * 6144 + NHT * P]
                zw = [rzz[zsl]] + ([rAT] if (nh == 0 and st < 3) else [])
                dma("SY", zz_ap.rearrange("p (k c) -> p k c", c=P),
                    zT_d[st], f"zz{zsl}", reads=[rzT], writes=zw)
                if DEBUG_DUMPS and nh == 0 and st < 2:
                    dma("GP", zdbg_d[st], zz_ap, "dbg", reads=[rzz[zsl]],
                        writes=[st_res(f"zdbg{st}")])

                pb = bank[0:2] if st % 2 == 0 else bank[2:4]
                pso = ps0[:, 0:1024] if st % 2 == 0 else ps0[:, 1024:2048]

                def dmm(e, _zz=zz_ap, _pso=pso):
                    for kt in range(NHT):
                        for n in range(2):
                            ins = e.matmul(
                                _pso[:, n * 512:(n + 1) * 512],
                                _zz[:, kt * P:(kt + 1) * P],
                                wd_ap(kt)[:, n * 512:(n + 1) * 512],
                                start=(kt == 0), stop=(kt == NHT - 1))
                    return ins
                tr.emit("PE", dmm, reads=[rzz[zsl]] + rwd, writes=pb)
                a_s1 = stcol_of(f"F3a_{st}_{nh}")
                a_s2 = stcol_of(f"F3b_{st}_{nh}")
                rs1 = st_res(f"F3a_{st}_{nh}")
                rs2 = st_res(f"F3b_{st}_{nh}")
                yw0 = [rytmp] + ([rAT] if (nh == 0 and st == 0) else [])
                tr.emit("AC", lambda e, _pso=pso: e.copy(y_tmp, _pso),
                        reads=pb, writes=yw0)
                tr.emit("DV", lambda e, _a=a_s1:
                        e.scalar_tensor_tensor(
                            y_half, y_tmp, 1.0, sdb_half, ALU.mult, ALU.mult,
                            accum_out=_a),
                        reads=[rytmp, smp[7]], writes=[rs1, smp[6]])
                jw = [rs2, rjunk_f3] + ([rAT] if (nh == 0 and st == 0)
                                        else [])
                tr.emit("DV", lambda e, _a=a_s2: e.scalar_tensor_tensor(
                    junk_f3, y_half, 1.0, y_half, ALU.mult, ALU.mult,
                    accum_out=_a),
                    reads=[smp[6]], writes=jw)
                dma("GP", y_d[st * P:(st + 1) * P,
                              nh * 1024:(nh + 1) * 1024], y_half,
                    f"yw{st}", reads=[smp[6]], writes=[ry[st][nh]])

                if nh == 1:
                    ysl = st % 2
                    xsl = 2 + st % 2
                    dma("AC", CFs(ysl), y_d[st * P:(st + 1) * P, :],
                        f"cf{ysl}", reads=[ry[st][0], ry[st][1]],
                        writes=[cfs[ysl]])
                    dma("SY", CFs(xsl), x1_d[st * P:(st + 1) * P, :],
                        f"cf{xsl}", reads=[rx1[st]], writes=[cfs[xsl]])
                    a_t1 = stcol_of("F4s1"); a_t2 = stcol_of("F4s2")
                    rt1 = st_res("F4s1"); rt2 = st_res("F4s2")
                    tr.emit("DV", lambda e, _st=st: e.tensor_tensor(
                        a_t1, stcol_of(f"F3a_{_st}_0"),
                        stcol_of(f"F3a_{_st}_1"), ALU.add),
                        reads=[st_res(f"F3a_{st}_0"), st_res(f"F3a_{st}_1")],
                        writes=[rt1])
                    tr.emit("DV", lambda e, _st=st: e.tensor_tensor(
                        a_t2, stcol_of(f"F3b_{_st}_0"),
                        stcol_of(f"F3b_{_st}_1"), ALU.add),
                        reads=[st_res(f"F3b_{st}_0"), st_res(f"F3b_{st}_1")],
                        writes=[rt2])
                    a_m = stcol_of("F4m"); a_mq = stcol_of("F4mq")
                    a_va = stcol_of("F4va"); a_lv = stcol_of("F4lv")
                    a_rs = stcol_of("F4rs"); a_nm = stcol_of("F4nm")
                    rm = st_res("F4m"); rmq = st_res("F4mq")
                    rva = st_res("F4va"); rlv = st_res("F4lv")
                    rrs = st_res("F4rs"); rnm = st_res("F4nm")
                    tr.emit("DV", lambda e: e.tensor_scalar(
                        a_m, a_t1, 1.0 / E, None, ALU.mult),
                        reads=[rt1], writes=[rm])
                    tr.emit("DV", lambda e: e.scalar_tensor_tensor(
                        a_mq, a_m, -1.0, a_m, ALU.mult, ALU.mult),
                        reads=[rm], writes=[rmq])
                    tr.emit("DV", lambda e: e.scalar_tensor_tensor(
                        a_va, a_t2, 1.0 / E, a_mq, ALU.mult, ALU.add),
                        reads=[rt2, rmq], writes=[rva])
                    tr.emit("AC", lambda e: e.activation(
                        a_lv, a_va, AF.Ln, bias=eps_t, scale=1.0),
                        reads=[rva, st_res("eps")], writes=[rlv])
                    tr.emit("AC", lambda e: e.activation(
                        a_rs, a_lv, AF.Exp, bias=0.0, scale=-0.5),
                        reads=[rlv], writes=[rrs])
                    tr.emit("DV", lambda e: e.scalar_tensor_tensor(
                        a_nm, a_m, -1.0, a_rs, ALU.mult, ALU.mult),
                        reads=[rm, rrs], writes=[rnm])
                    tr.emit("DV", lambda e, _ysl=ysl: e.tensor_scalar(
                        CFs(_ysl), CFs(_ysl), a_rs, a_nm, ALU.mult, ALU.add),
                        reads=[rrs, rnm, cfs[ysl]], writes=[cfs[ysl]])
                    tr.emit("DV", lambda e, _ysl=ysl, _xsl=xsl:
                            e.tensor_tensor(CFs(_ysl), CFs(_ysl), CFs(_xsl),
                                            ALU.add),
                            reads=[cfs[ysl], cfs[xsl]], writes=[cfs[ysl]])
                    d = dma("GP", out_d[st * P:(st + 1) * P, :], CFs(ysl),
                            "outw", reads=[cfs[ysl]],
                            writes=[st_res(f"out{st}")])
                    outsd.append(d)

        tr.raw_wait("GP", outsd[-1])
        if DEBUG_SCRATCH:
            pass

        for en in tr.e.values():
            en.sem_c = E_(nc.semaphore(f"sem_{en.name}"))
        for sname, sd in tr.streams.items():
            sd[0] = E_(nc.semaphore(f"semd_{sname}"))

        with nc.Block() as block:
            @block.sync
            def _(eng):
                for t in tr.e["SY"].thunks:
                    t(eng)

            @block.gpsimd
            def _(eng):
                for t in tr.e["GP"].thunks:
                    t(eng)

            @block.vector
            def _(eng):
                for t in tr.e["DV"].thunks:
                    t(eng)

            @block.scalar
            def _(eng):
                for t in tr.e["AC"].thunks:
                    t(eng)

            @block.tensor
            def _(eng):
                for t in tr.e["PE"].thunks:
                    t(eng)

    return nc


_PROG_CACHE = {}


def _get_program():
    if "nc" not in _PROG_CACHE:
        _PROG_CACHE["nc"] = build_program()
    return _PROG_CACHE["nc"]


def kernel(L0_attn_x, L0_attn_w_qkv, L0_attn_w_out, L0_ffn_wq_gate,
           L0_ffn_s_gate, L0_ffn_wq_up, L0_ffn_s_up, L0_ffn_wq_down,
           L0_ffn_s_down):
    x = np.ascontiguousarray(np.asarray(L0_attn_x, dtype=np.float32))
    wqkvT = np.ascontiguousarray(np.asarray(L0_attn_w_qkv, np.float32).T)
    woutT = np.ascontiguousarray(np.asarray(L0_attn_w_out, np.float32).T)
    wgT = np.ascontiguousarray(np.asarray(L0_ffn_wq_gate, np.int32).T
                               .astype(np.uint8))
    wuT = np.ascontiguousarray(np.asarray(L0_ffn_wq_up, np.int32).T
                               .astype(np.uint8))
    wdT = np.ascontiguousarray(np.asarray(L0_ffn_wq_down, np.int32).T
                               .astype(np.uint8))
    sgv = np.ascontiguousarray(
        np.asarray(L0_ffn_s_gate, np.float32).reshape(NHT, P).T)
    suv = np.ascontiguousarray(
        np.asarray(L0_ffn_s_up, np.float32).reshape(NHT, P).T)
    sd = np.ascontiguousarray(
        np.asarray(L0_ffn_s_down, np.float32).reshape(1, E))

    ii = np.arange(C, dtype=np.int64)[:, None]
    jj = np.arange(2 * C, dtype=np.int64)[None, :]
    allowed = (jj >= ii) & (jj <= ii + C)
    maskq = np.where(allowed, np.float32(0.0),
                     np.float32(MASKVAL)).astype(np.float32)
    ident = np.eye(P, dtype=np.float32)

    nc = _get_program()
    shared = dict(wqkvT=wqkvT, woutT=woutT, wgT=wgT, wuT=wuT, wdT=wdT,
                  sgv=sgv, suv=suv, sd=sd, maskq=maskq, ident=ident)
    in_maps = [dict(x0=np.ascontiguousarray(x[c]), **shared)
               for c in range(8)]
    res = run_bass_kernel_spmd(nc, in_maps, core_ids=list(range(8)))
    return np.stack([r["out"] for r in res.results], axis=0)


# version 24
# speedup vs baseline: 1.1608x; 1.1608x over previous
import contextlib
import numpy as np
import concourse.bass as bass
import concourse.mybir as mybir
from concourse.bass_utils import run_bass_kernel_spmd

F32 = mybir.dt.float32
BF16 = mybir.dt.bfloat16
I32 = mybir.dt.int32
U8 = mybir.dt.uint8
AF = mybir.ActivationFunctionType
ALU = mybir.AluOpType
AX = mybir.AxisListType

P = 128
S = 2048
E = 2048
NH = 16
D = 128
C = 512
NCH = S // C
HID = 5632
NKE = E // P
NST = S // P
NHT = HID // P
EPS = 1e-5
SCALE = float(np.float64(1.0) / np.float64(np.sqrt(np.float64(D))))
MASKVAL = -3.0e9

DEBUG_SCRATCH = False
DEBUG_DUMPS = False
BISECT_IMMEDIATE_FLUSH = False


class _Eng:
    def __init__(self, name):
        self.name = name
        self.thunks = []
        self.n_c = 0
        self.waited = {}
        self.sem_c = None


class Res:
    __slots__ = ("name", "w", "r")

    def __init__(self, name):
        self.name = name
        self.w = None
        self.r = {}


class TR:

    def __init__(self):
        self.e = {n: _Eng(n) for n in ("PE", "DV", "AC", "GP", "SY")}
        self.streams = {}

    def stream(self, name):
        if name not in self.streams:
            self.streams[name] = [None, 0]
        return self.streams[name]

    def emit(self, eng, fn, reads=(), writes=(), dma=None):
        st = self.e[eng]
        waits = []

        def need(dep):
            if dep is None:
                return
            semkey, val = dep
            if semkey[0] == "D":
                val = self.streams[semkey[1]][1]
            got = st.waited.get(semkey, 0)
            if got >= val:
                return
            st.waited[semkey] = val
            waits.append((semkey, val))

        for r in reads:
            need(r.w)
        for w in writes:
            need(w.w)
            for dep in w.r.values():
                need(dep)

        if dma is not None:
            key = f"{eng}:{dma}"
            sd = self.stream(key)
            sd[1] += 16
            dep = (("D", key), sd[1])
            inc_amt = 16
        else:
            st.n_c += 1
            dep = (("C", eng), st.n_c)
            inc_amt = 1

        def thunk(engobj, _waits=tuple(waits), _fn=fn, _dep=dep,
                  _inc=inc_amt, _tr=self):
            for (semkey, val) in _waits:
                engobj.wait_ge(_tr._sem(semkey), val)
            last = _fn(engobj)
            last.then_inc(_tr._sem(_dep[0]), _inc)

        st.thunks.append(thunk)

        for r in reads:
            r.r[eng] = dep
        for w in writes:
            w.w = dep
            w.r = {}
        return dep

    def _sem(self, semkey):
        if semkey[0] == "D":
            return self.streams[semkey[1]][0]
        return self.e[semkey[1]].sem_c

    def raw_wait(self, eng, dep):
        st = self.e[eng]
        semkey, val = dep

        def thunk(engobj, _semkey=semkey, _val=val, _tr=self):
            engobj.wait_ge(_tr._sem(_semkey), _val)

        st.thunks.append(thunk)


def build_program():
    nc = bass.Bass("TRN2", target_bir_lowering=False, debug=False)

    x0_d = nc.dram_tensor("x0", [S, E], F32, kind="ExternalInput").ap()
    wqkvT_d = nc.dram_tensor("wqkvT", [E, 3 * E], F32, kind="ExternalInput").ap()
    woutT_d = nc.dram_tensor("woutT", [E, E], F32, kind="ExternalInput").ap()
    wgT_d = nc.dram_tensor("wgT", [E, HID], U8, kind="ExternalInput").ap()
    wuT_d = nc.dram_tensor("wuT", [E, HID], U8, kind="ExternalInput").ap()
    wdT_d = nc.dram_tensor("wdT", [HID, E], U8, kind="ExternalInput").ap()
    sgv_d = nc.dram_tensor("sgv", [P, NHT], F32, kind="ExternalInput").ap()
    suv_d = nc.dram_tensor("suv", [P, NHT], F32, kind="ExternalInput").ap()
    sd_d = nc.dram_tensor("sd", [1, E], F32, kind="ExternalInput").ap()
    maskq_d = nc.dram_tensor("maskq", [C, 2 * C], F32, kind="ExternalInput").ap()
    ident_d = nc.dram_tensor("ident", [P, P], F32, kind="ExternalInput").ap()
    out_d = nc.dram_tensor("out", [S, E], F32, kind="ExternalOutput").ap()

    scratch_kind = "ExternalOutput" if DEBUG_SCRATCH else "Internal"
    x1_d = nc.dram_tensor("x1s", [S, E], F32, kind=scratch_kind).ap()
    atT_d = nc.dram_tensor("atTs", [NST, P, NH, P], BF16, kind=scratch_kind).ap()
    zT_d = nc.dram_tensor("zTs", [NST, P, NHT, P], BF16, kind=scratch_kind).ap()
    y_d = nc.dram_tensor("ys", [S, E], F32, kind=scratch_kind).ap()
    rec_d = nc.dram_tensor("recs", [NST, E], F32, kind="Internal").ap()
    if DEBUG_SCRATCH:
        wdbg_d = nc.dram_tensor("wdbg", [2, NHT, P, 1024], BF16,
                                kind="ExternalOutput").ap()
        zdbg_d = nc.dram_tensor("zdbg", [2, P, NHT * P], BF16,
                                kind="ExternalOutput").ap()

    tr = TR()
    ctx = contextlib.ExitStack()

    with ctx:
        E_ = ctx.enter_context
        AT = E_(nc.sbuf_tensor("AT", [P, 16 * 2048], BF16))
        BB = E_(nc.sbuf_tensor("BB", [P, 16 * 2048], BF16))
        SM = E_(nc.sbuf_tensor("SM", [P, 8 * 2048], BF16))
        CF = E_(nc.sbuf_tensor("CF", [P, 4 * 2048], F32))
        MK = E_(nc.sbuf_tensor("MK", [P, 4 * 1024], BF16))
        IDB = E_(nc.sbuf_tensor("IDB", [P, P], BF16))
        IDF = E_(nc.sbuf_tensor("IDF", [P, P], F32))
        AB = E_(nc.sbuf_tensor("AB", [P, 1024], BF16))
        STT = E_(nc.sbuf_tensor("STT", [P, 256], F32))
        DEN = E_(nc.sbuf_tensor("DEN", [P, 256], F32))
        DENR = E_(nc.sbuf_tensor("DENR", [P, 256], F32))
        RCS = E_(nc.sbuf_tensor("RCS", [P, P], F32))

        ps0 = E_(nc.psum_tensor("ps0", [P, 2048], F32))
        ps1 = E_(nc.psum_tensor("ps1", [P, 2048], F32))
        ps1b = ps1.bitcast(BF16)

        bank = [Res(f"bank{i}") for i in range(8)]
        rAT = Res("AT")
        bbp = [Res(f"BB{i}") for i in range(16)]
        smp = [Res(f"SM{i}") for i in range(8)]
        cfs = [Res(f"CF{i}") for i in range(4)]
        rMK = Res("MK")
        rIDB = Res("IDB")
        rIDF = Res("IDF")
        rABs = [Res("AB0"), Res("AB1")]
        stc = {}

        def st_res(name):
            if name not in stc:
                stc[name] = Res(name)
            return stc[name]

        rx1 = [Res(f"x1_{i}") for i in range(NST)]
        rDEN = Res("DEN")
        rDENR = Res("DENR")
        rRCS = Res("RCS")
        rREC = [Res(f"RECscr{i}") for i in range(NST)]
        ratT = Res("atT")
        rzT = Res("zT")
        ry = [[Res(f"y_{i}_{h}") for h in range(2)] for i in range(NST)]

        _stcol = [0]

        def stcol(n=1):
            c = _stcol[0]
            _stcol[0] += n
            assert _stcol[0] <= 256
            return STT[:, c:c + n]

        sgv_t = stcol(NHT)
        suv_t = stcol(NHT)
        eps_t = stcol(1)
        m128_t = stcol(1)

        def BBpg(i, n=1):
            return BB[:, i * 2048:(i + n) * 2048]

        def SMpg(i, n=1):
            return SM[:, i * 2048:(i + n) * 2048]

        SMf = SM.bitcast(F32)

        def SMf_pg(i, n=1):
            return SMf[:, i * 1024:(i + n) * 1024]

        def CFs(i, n=1):
            return CF[:, i * 2048:(i + n) * 2048]

        CFi = CF.bitcast(I32)

        def CFi_s(i, n=1):
            return CFi[:, i * 2048:(i + n) * 2048]

        CFu = CF.bitcast(U8)

        def CFu_s(i):
            return CFu[:, i * 8192:i * 8192 + 2048]

        def dma(eng, dst, src, stream, reads=(), writes=()):
            return tr.emit(eng, lambda e, _d=dst, _s=src: e.dma_start(_d, _s),
                           reads=reads, writes=writes, dma=stream)

        def ln_stats(pref, src_ap, width, src_reads, junk_ap, junk_res,
                     extra_sum=None):
            s1 = st_res(pref + "s1"); s2 = st_res(pref + "s2")
            m = st_res(pref + "m"); mq = st_res(pref + "mq")
            va = st_res(pref + "va"); lv = st_res(pref + "lv")
            rs = st_res(pref + "rs"); nm = st_res(pref + "nm")
            a_s1 = stcol_of(pref + "s1"); a_s2 = stcol_of(pref + "s2")
            a_m = stcol_of(pref + "m"); a_mq = stcol_of(pref + "mq")
            a_va = stcol_of(pref + "va"); a_lv = stcol_of(pref + "lv")
            a_rs = stcol_of(pref + "rs"); a_nm = stcol_of(pref + "nm")

            tr.emit("DV", lambda e: e.reduce_sum(a_s1, src_ap, axis=AX.X),
                    reads=src_reads, writes=[s1])
            tr.emit("DV", lambda e: e.scalar_tensor_tensor(
                junk_ap, src_ap, 1.0, src_ap, ALU.mult, ALU.mult,
                accum_out=a_s2), reads=src_reads, writes=[s2, junk_res])
            tr.emit("DV", lambda e: e.tensor_scalar(
                a_m, a_s1, 1.0 / width, None, ALU.mult),
                reads=[s1], writes=[m])
            tr.emit("DV", lambda e: e.scalar_tensor_tensor(
                a_mq, a_m, -1.0, a_m, ALU.mult, ALU.mult),
                reads=[m], writes=[mq])
            tr.emit("DV", lambda e: e.scalar_tensor_tensor(
                a_va, a_s2, 1.0 / width, a_mq, ALU.mult, ALU.add),
                reads=[s2, mq], writes=[va])
            tr.emit("AC", lambda e: e.activation(
                a_lv, a_va, AF.Ln, bias=eps_t, scale=1.0),
                reads=[va, st_res("eps")], writes=[lv])
            tr.emit("AC", lambda e: e.activation(
                a_rs, a_lv, AF.Exp, bias=0.0, scale=-0.5),
                reads=[lv], writes=[rs])
            tr.emit("DV", lambda e: e.scalar_tensor_tensor(
                a_nm, a_m, -1.0, a_rs, ALU.mult, ALU.mult),
                reads=[m, rs], writes=[nm])
            return a_rs, a_nm, rs, nm

        _st_aps = {}

        def stcol_of(name, n=1):
            if name not in _st_aps:
                _st_aps[name] = stcol(n)
            return _st_aps[name]

        tr.emit("DV", lambda e: e.memset(eps_t, EPS), writes=[st_res("eps")])
        tr.emit("DV", lambda e: e.memset(m128_t, -128.0),
                writes=[st_res("m128")])
        dma("SY", sgv_t, sgv_d, "sg", writes=[st_res("sgv")])
        dma("SY", suv_t, suv_d, "su", writes=[st_res("suv")])
        dma("SY", CFs(0)[:, 0:P], ident_d, "cf0", writes=[cfs[0]])
        tr.emit("DV", lambda e: e.tensor_copy(IDB[:], CFs(0)[:, 0:P]),
                reads=[cfs[0]], writes=[rIDB])
        tr.emit("DV", lambda e: e.tensor_copy(IDF[:], CFs(0)[:, 0:P]),
                reads=[cfs[0]], writes=[rIDF])
        dma("SY", CFs(2, 2).rearrange("p (t c) -> p t c", c=1024),
            maskq_d.rearrange("(t p) c -> p t c", p=P),
            "cf2", writes=[cfs[2], cfs[3]])
        tr.emit("DV", lambda e: e.tensor_copy(MK[:], CFs(2, 2)),
                reads=[cfs[2], cfs[3]], writes=[rMK])

        def mask_tile(t):
            return MK[:, t * 1024:(t + 1) * 1024]

        for st in range(NST):
            sl = st % 2
            dma("GP", CFs(sl), x0_d[st * P:(st + 1) * P, :], f"cf{sl}",
                writes=[cfs[sl]])
            rstd, negmr, r_rs, r_nm = ln_stats(
                "A0", CFs(sl), E, [cfs[sl]], BBpg(0), bbp[0])
            hp = st % 2
            tr.emit("DV", lambda e, _sl=sl, _hp=hp, _rs=rstd, _nm=negmr:
                    e.tensor_scalar(SMpg(_hp), CFs(_sl), _rs, _nm,
                                    ALU.mult, ALU.add),
                    reads=[cfs[sl], r_rs, r_nm], writes=[smp[hp]])
            boff = 0 if st % 2 == 0 else 2048
            bks = [bank[4], bank[5]] if st % 2 == 0 else [bank[6], bank[7]]

            def tp_fn(e, _hp=hp, _boff=boff):
                for k in range(NKE):
                    ins = e.transpose(
                        ps1b[:, _boff + k * P:_boff + (k + 1) * P],
                        SMpg(_hp)[:, k * P:(k + 1) * P], IDB[:])
                return ins
            tr.emit("PE", tp_fn, reads=[smp[hp], rIDB], writes=bks)

            def ev_fn(e, _st=st, _boff=boff):
                dst = AT[:].rearrange("p (k c) -> p k c", c=2048)[
                    :, :, _st * P:(_st + 1) * P]
                src = ps1b[:, _boff:_boff + 2048].rearrange(
                    "p (k c) -> p k c", c=P)
                return e.copy(dst, src)
            tr.emit("AC", ev_fn, reads=bks, writes=[rAT])

        def h1T(k, lo, n):
            return AT[:, k * 2048 + lo:k * 2048 + lo + n]

        pe_fill = []

        def drain_fill(n):
            for _ in range(min(n, len(pe_fill))):
                pe_fill.pop(0)()

        def stage_v_group(g):
            dma("SY", CF[:].rearrange("p (k c) -> p k c", c=512),
                wqkvT_d.rearrange("(k p) f -> p k f", p=P)[
                    :, :, 2 * E + g * 512:2 * E + (g + 1) * 512],
                "cf0", writes=cfs)
            tr.emit("DV", lambda e: e.tensor_copy(BBpg(4, 4), CF[:]),
                    reads=cfs, writes=[bbp[4], bbp[5], bbp[6], bbp[7]])

        def enq_v_mms():
            for q4 in range(4):
                def unit(_q4=q4):
                    for vst in range(_q4 * 4, _q4 * 4 + 4):
                        vb = bank[4 + vst % 2]
                        pso = ps1[:, (vst % 2) * 512:(vst % 2) * 512 + 512]

                        def vmm(e, _vst=vst, _pso=pso):
                            for k in range(NKE):
                                ins = e.matmul(
                                    _pso,
                                    AT[:, k * 2048 + _vst * P:
                                       k * 2048 + (_vst + 1) * P],
                                    BB[:, 4 * 2048 + k * 512:
                                       4 * 2048 + (k + 1) * 512],
                                    start=(k == 0), stop=(k == NKE - 1))
                            return ins
                        tr.emit("PE", vmm,
                                reads=[rAT, bbp[4], bbp[5], bbp[6], bbp[7]],
                                writes=[vb])
                        tr.emit("AC", lambda e, _vst=vst, _pso=pso: e.copy(
                            BB[:, (_vst // 4) * 2048 + (_vst % 4) * 512:
                               (_vst // 4) * 2048 + (_vst % 4) * 512 + 512],
                            _pso), reads=[vb], writes=[bbp[vst // 4]])
                pe_fill.append(unit)

        def stage_qk(hh):
            par = hh % 2
            for (slot, fcol, wpg, wap) in (
                    (par * 2, hh * P, bbp[8 + 2 * par], BBpg(8 + 2 * par)),
                    (par * 2 + 1, E + hh * P, bbp[9 + 2 * par],
                     BBpg(9 + 2 * par))):
                stg = cfs[slot]
                stg_ap = CFs(slot)
                dma("SY", stg_ap.rearrange("p (k c) -> p k c", c=P),
                    wqkvT_d.rearrange("(k p) f -> p k f", p=P)[
                        :, :, fcol:fcol + P],
                    f"cf{slot}", writes=[stg])
                tr.emit("DV", lambda e, _w=wap, _s=stg_ap:
                        e.tensor_copy(_w, _s), reads=[stg], writes=[wpg])

        def enq_qk_mms(hh):
            par = hh % 2
            for (isq, wpg, wap, opg, oap) in (
                    (True, bbp[8 + 2 * par], BBpg(8 + 2 * par),
                     bbp[12 + 2 * par], BBpg(12 + 2 * par)),
                    (False, bbp[9 + 2 * par], BBpg(9 + 2 * par),
                     bbp[13 + 2 * par], BBpg(13 + 2 * par))):
                for half in range(2):
                    for n in range(2):
                        def unit(_wpg=wpg, _wap=wap, _opg=opg, _oap=oap,
                                 _half=half, _n=n, _isq=isq):
                            def qkmm(e, _w=_wap, _h2=_half, _nn=_n):
                                for k in range(NKE):
                                    ins = e.matmul(
                                        ps1[:, _nn * 512:(_nn + 1) * 512],
                                        _w[:, k * P:(k + 1) * P],
                                        h1T(k, _h2 * 1024 + _nn * 512, 512),
                                        start=(k == 0), stop=(k == NKE - 1))
                                return ins
                            tr.emit("PE", qkmm, reads=[_wpg, rAT],
                                    writes=[bank[4 + _n]])
                            if _n == 1:
                                sc = SCALE if _isq else 1.0
                                tr.emit("AC", lambda e, _o=_oap, _h2=_half,
                                        _sc=sc:
                                        e.activation(
                                            _o[:, _h2 * 1024:
                                               (_h2 + 1) * 1024],
                                            ps1[:, 0:1024], AF.Copy,
                                            bias=0.0, scale=_sc),
                                        reads=[bank[4], bank[5]],
                                        writes=[_opg])
                        pe_fill.append(unit)

        stage_v_group(0)
        stage_qk(0)
        enq_v_mms()
        enq_qk_mms(0)
        drain_fill(99)

        for h in range(NH):
            par = h % 2
            if h + 1 < NH:
                stage_qk(h + 1)
                enq_qk_mms(h + 1)

            def vtile(sk_t, hh):
                return BB[:, (sk_t // 4) * 2048 + (sk_t % 4) * 512
                          + (hh % 4) * P:
                          (sk_t // 4) * 2048 + (sk_t % 4) * 512
                          + (hh % 4) * P + P]

            qT_pg, kT_pg = bbp[12 + 2 * par], bbp[13 + 2 * par]
            pend_tp = [None]

            def flush_tp():
                if pend_tp[0] is not None:
                    pend_tp[0]()
                    pend_tp[0] = None

            for chn in range(NCH):
                nsk = C if chn == 0 else 2 * C
                ksk0 = (chn - 1) * C if chn > 0 else 0
                nskt = nsk // P
                cpar = chn % 2
                probs_pg = [smp[2 * cpar], smp[2 * cpar + 1]]
                probs_ap = SMpg(2 * cpar, 2)
                probsT_pg = [smp[4 + 2 * cpar], smp[5 + 2 * cpar]]
                probsT_ap = SMpg(4 + 2 * cpar, 2)

                for t in range(4):
                    sq0 = chn * C + t * P
                    reg = (t % 2) * 1024
                    sc_bks = [bank[2 * (t % 2)], bank[2 * (t % 2) + 1]]

                    def smm(e, _t=t, _sq0=sq0, _chn=chn, _par=par,
                            _nsk=nsk, _ksk0=ksk0, _reg=reg):
                        scp = ps0[:, _reg:_reg + _nsk]
                        mkap = (mask_tile(_t)[:, 512:1024] if _chn == 0
                                else mask_tile(_t))
                        for half in range(_nsk // 512):
                            e.matmul(
                                scp[:, half * 512:(half + 1) * 512],
                                BB[:, (12 + 2 * _par) * 2048 + _sq0:
                                   (12 + 2 * _par) * 2048 + _sq0 + P],
                                BB[:, (13 + 2 * _par) * 2048 + _ksk0
                                   + half * 512:
                                   (13 + 2 * _par) * 2048 + _ksk0
                                   + half * 512 + 512],
                                start=True, stop=False)
                            ins = e.matmul(
                                scp[:, half * 512:(half + 1) * 512],
                                IDB[:],
                                mkap[:, half * 512:(half + 1) * 512],
                                start=False, stop=True)
                        return ins
                    tr.emit("PE", smm,
                            reads=[qT_pg, kT_pg, rMK, rIDB], writes=sc_bks)

                    nbr = st_res(f"nb{par}_{cpar}_{t}")
                    a_nb = stcol_of(f"nb{par}_{cpar}_{t}")
                    den_ap = DEN[:, h * 16 + chn * 4 + t:
                                 h * 16 + chn * 4 + t + 1]

                    tr.emit("DV", lambda e, _nsk=nsk, _a=a_nb, _reg=reg:
                            e.reduce_max(_a, ps0[:, _reg:_reg + _nsk],
                                         axis=AX.X, negate=True),
                            reads=sc_bks, writes=[nbr])
                    pr_ap = probs_ap[:, t * nsk:t * nsk + nsk]
                    pgi = (t * nsk) // 2048
                    tr.emit("AC", lambda e, _nsk=nsk, _pr=pr_ap, _reg=reg,
                            _nb=a_nb, _dn=den_ap:
                            e.activation(_pr, ps0[:, _reg:_reg + _nsk],
                                         AF.Exp, bias=_nb, scale=1.0,
                                         accum_out=_dn),
                            reads=sc_bks + [nbr],
                            writes=[rDEN, probs_pg[pgi]])
                    if t == 1:
                        drain_fill(1)

                flush_tp()
                drain_fill(1)

                def mk_tp(_chn=chn, _nsk=nsk, _nskt=nskt, _par=par,
                          _h=h, _probs_ap=probs_ap,
                          _probs_pg=tuple(probs_pg),
                          _probsT_ap=probsT_ap,
                          _probsT_pg=tuple(probsT_pg)):
                    def run():
                        npair = _nskt // 2
                        pv_ps = ps1[:, 1536:2048]
                        pv_bk = bank[7]
                        for pair in range(npair):
                            ptb = 6 + pair % 2
                            pt_ps = ps1b[:, (ptb - 4) * 1024:
                                         (ptb - 3) * 1024]
                            pt_bk = bank[ptb]

                            def tpf(e, _pair=pair, _pt=pt_ps):
                                for t in range(4):
                                    for j in range(2):
                                        skt = _pair * 2 + j
                                        ins = e.transpose(
                                            _pt[:, j * 512 + t * P:
                                                j * 512 + (t + 1) * P],
                                            _probs_ap[:, t * _nsk + skt * P:
                                                      t * _nsk
                                                      + (skt + 1) * P],
                                            IDB[:])
                                return ins
                            tr.emit("PE", tpf,
                                    reads=[_probs_pg[0], _probs_pg[1],
                                           rIDB],
                                    writes=[pt_bk])
                            tr.emit("DV", lambda e, _pair=pair, _pt=pt_ps:
                                    e.tensor_copy(
                                        _probsT_ap[:, _pair * 1024:
                                                   (_pair + 1) * 1024],
                                        _pt),
                                    reads=[pt_bk],
                                    writes=[_probsT_pg[pair // 2]])

                        def pvf(e):
                            for skt in range(_nskt):
                                glob_skt = ((0 if _chn == 0
                                             else (_chn - 1) * 4) + skt)
                                ins = e.matmul(
                                    pv_ps, vtile(glob_skt, _h),
                                    _probsT_ap[:, (skt // 2) * 1024
                                               + (skt % 2) * 512:
                                               (skt // 2) * 1024
                                               + (skt % 2) * 512 + 512],
                                    start=(skt == 0),
                                    stop=(skt == _nskt - 1))
                            return ins
                        tr.emit("PE", pvf,
                                reads=[_probsT_pg[0], _probsT_pg[1],
                                       bbp[0], bbp[1], bbp[2], bbp[3]],
                                writes=[pv_bk])
                        ab = rABs[_chn % 2]
                        ab_ap = AB[:, (_chn % 2) * 512:
                                   (_chn % 2) * 512 + 512]
                        tr.emit("AC", lambda e, _ab=ab_ap:
                                e.copy(_ab, pv_ps), reads=[pv_bk],
                                writes=[ab])
                        dma("GP",
                            atT_d[_chn * 4:(_chn + 1) * 4, :, _h, :]
                            .transpose([1, 0, 2]),
                            ab_ap.rearrange("p (t c) -> p t c", c=P),
                            "atw", reads=[ab], writes=[ratT])
                    return run
                pend_tp[0] = mk_tp()
                if BISECT_IMMEDIATE_FLUSH:
                    flush_tp()

            flush_tp()
            drain_fill(1)

            if h + 1 < NH and (h + 1) % 4 == 0:
                stage_v_group((h + 1) // 4)
                enq_v_mms()
            drain_fill(2)
        drain_fill(99)

        for k in range(NKE):
            sl = 2 + (k % 2)
            dma("SY", CFs(sl), woutT_d[k * P:(k + 1) * P, :], f"cf{sl}",
                writes=[cfs[sl]])
            tr.emit("DV", lambda e, _k=k, _sl=sl:
                    e.tensor_copy(BBpg(_k), CFs(_sl)),
                    reads=[cfs[sl]], writes=[bbp[k]])

        y_sb = SMf_pg(0, 2)
        x1_sb = SMf_pg(2, 2)
        pend_tp2 = [None]

        tr.emit("DV", lambda e: e.reciprocal(DENR[:], DEN[:]),
                reads=[rDEN], writes=[rDENR])
        def rec_build(st2):
            base = (st2 // 4) * 4 + st2 % 4
            q = st2 % 4
            tp_ps = ps1[0:16, q * 128:q * 128 + 128]

            def rtp(e, _b=base, _p=tp_ps):
                return e.transpose(_p, DENR[:, _b::16], IDF[:])
            tr.emit("PE", rtp, reads=[rDENR, rIDF], writes=[bank[4]])
            tr.emit("AC", lambda e, _p=tp_ps: e.copy(RCS[0:16, :], _p),
                    reads=[bank[4]], writes=[rRCS])
            dma("GP", rec_d[st2].rearrange("(a c) -> a c", a=16),
                RCS[0:16, :], "recw", reads=[rRCS],
                writes=[rREC[st2]])

        def at_load(st2):
            asl2 = 4 + st2 % 2
            dma("AC", SMpg(asl2), atT_d[st2].rearrange("p h c -> p (h c)"),
                f"sm{asl2}", reads=[ratT], writes=[smp[asl2]])
            csl = 2 + st2 % 2
            dma("AC", CFs(csl),
                rec_d[st2].rearrange("(o c) -> o c", o=1)
                .partition_broadcast(P),
                f"cf{csl}", reads=[rREC[st2]], writes=[cfs[csl]])
            tr.emit("DV", lambda e, _a=SMpg(asl2), _c=CFs(csl):
                    e.tensor_tensor(_a, _a, _c, ALU.mult),
                    reads=[smp[asl2], cfs[csl]],
                    writes=[smp[asl2]])

        rec_build(0)
        rec_build(1)
        rec_build(2)
        at_load(0)
        at_load(1)
        for st in range(NST):
            asl = 4 + st % 2
            dma("GP", CFs(st % 2), x0_d[st * P:(st + 1) * P, :],
                f"cf{st % 2}", writes=[cfs[st % 2]])

            def projmm(e, _asl=asl):
                for n in range(4):
                    for k in range(NKE):
                        ins = e.matmul(
                            ps0[:, n * 512:(n + 1) * 512],
                            SM[:, _asl * 2048 + k * P:
                               _asl * 2048 + (k + 1) * P],
                            BB[:, k * 2048 + n * 512:k * 2048 + n * 512 + 512],
                            start=(k == 0), stop=(k == NKE - 1))
                return ins
            tr.emit("PE", projmm,
                    reads=[smp[asl]] + bbp[0:16], writes=bank[0:4])
            if st + 3 < NST:
                rec_build(st + 3)
            if st + 2 < NST:
                at_load(st + 2)
            if pend_tp2[0] is not None:
                pend_tp2[0]()
                pend_tp2[0] = None
            s1p = [st_res(f"A3p{i}") for i in range(4)]
            for n in range(4):
                a_p = stcol_of(f"A3p{n}")
                tr.emit("AC", lambda e, _n=n, _a=a_p: e.activation(
                    y_sb[:, _n * 512:(_n + 1) * 512],
                    ps0[:, _n * 512:(_n + 1) * 512],
                    AF.Copy, bias=0.0, scale=1.0, accum_out=_a),
                    reads=[bank[n]], writes=[s1p[n], smp[n // 2]])
            rs1 = st_res("A3s1")
            a_s1 = stcol_of("A3s1")
            tr.emit("DV", lambda e: e.tensor_tensor(
                a_s1, stcol_of("A3p0"), stcol_of("A3p1"), ALU.add),
                reads=[s1p[0], s1p[1]], writes=[rs1])
            tr.emit("DV", lambda e: e.tensor_tensor(
                a_s1, a_s1, stcol_of("A3p2"), ALU.add),
                reads=[rs1, s1p[2]], writes=[rs1])
            tr.emit("DV", lambda e: e.tensor_tensor(
                a_s1, a_s1, stcol_of("A3p3"), ALU.add),
                reads=[rs1, s1p[3]], writes=[rs1])
            rs2 = st_res("A3s2")
            a_s2 = stcol_of("A3s2")
            tr.emit("DV", lambda e: e.scalar_tensor_tensor(
                SMpg(7), y_sb, 1.0, y_sb, ALU.mult, ALU.mult,
                accum_out=a_s2),
                reads=[smp[0], smp[1]], writes=[rs2, smp[7]])
            a_m = stcol_of("A3m"); a_mq = stcol_of("A3mq")
            a_va = stcol_of("A3va"); a_lv = stcol_of("A3lv")
            a_rs = stcol_of("A3rs"); a_nm = stcol_of("A3nm")
            rm = st_res("A3m"); rmq = st_res("A3mq"); rva = st_res("A3va")
            rlv = st_res("A3lv"); rrs = st_res("A3rs"); rnm = st_res("A3nm")
            tr.emit("DV", lambda e: e.tensor_scalar(
                a_m, a_s1, 1.0 / E, None, ALU.mult), reads=[rs1], writes=[rm])
            tr.emit("DV", lambda e: e.scalar_tensor_tensor(
                a_mq, a_m, -1.0, a_m, ALU.mult, ALU.mult),
                reads=[rm], writes=[rmq])
            tr.emit("DV", lambda e: e.scalar_tensor_tensor(
                a_va, a_s2, 1.0 / E, a_mq, ALU.mult, ALU.add),
                reads=[rs2, rmq], writes=[rva])
            tr.emit("AC", lambda e: e.activation(
                a_lv, a_va, AF.Ln, bias=eps_t, scale=1.0),
                reads=[rva, st_res("eps")], writes=[rlv])
            tr.emit("AC", lambda e: e.activation(
                a_rs, a_lv, AF.Exp, bias=0.0, scale=-0.5),
                reads=[rlv], writes=[rrs])
            tr.emit("DV", lambda e: e.scalar_tensor_tensor(
                a_nm, a_m, -1.0, a_rs, ALU.mult, ALU.mult),
                reads=[rm, rrs], writes=[rnm])
            tr.emit("DV", lambda e: e.tensor_scalar(
                y_sb, y_sb, a_rs, a_nm, ALU.mult, ALU.add),
                reads=[rrs, rnm, smp[0], smp[1]], writes=[smp[0], smp[1]])
            tr.emit("DV", lambda e, _sl=st % 2: e.tensor_tensor(
                x1_sb, y_sb, CFs(_sl), ALU.add),
                reads=[smp[0], smp[1], cfs[st % 2]], writes=[smp[2], smp[3]])
            dma("GP", x1_d[st * P:(st + 1) * P, :], x1_sb,
                f"x1w{st}", reads=[smp[2], smp[3]], writes=[rx1[st]])
            rstd2, negmr2, r_rs2, r_nm2 = ln_stats(
                "A3b", x1_sb, E, [smp[2], smp[3]], SMpg(7), smp[7])
            tr.emit("DV", lambda e, _rs=rstd2, _nm=negmr2: e.tensor_scalar(
                SMpg(6), x1_sb, _rs, _nm, ALU.mult, ALU.add),
                reads=[smp[2], smp[3], r_rs2, r_nm2], writes=[smp[6]])
            boff = 0 if st % 2 == 0 else 2048
            bks = [bank[4], bank[5]] if st % 2 == 0 else [bank[6], bank[7]]

            def mk_pend(_st=st, _boff=boff, _bks=bks):
                def run():
                    def tp2(e):
                        for k in range(NKE):
                            ins = e.transpose(
                                ps1b[:, _boff + k * P:_boff + (k + 1) * P],
                                SMpg(6)[:, k * P:(k + 1) * P], IDB[:])
                        return ins
                    tr.emit("PE", tp2, reads=[smp[6], rIDB], writes=_bks)

                    def ev2(e):
                        dst = AT[:].rearrange("p (k c) -> p k c", c=2048)[
                            :, :, _st * P:(_st + 1) * P]
                        srcp = ps1b[:, _boff:_boff + 2048].rearrange(
                            "p (k c) -> p k c", c=P)
                        return e.copy(dst, srcp)
                    tr.emit("AC", ev2, reads=_bks, writes=[rAT])
                return run
            pend_tp2[0] = mk_pend()

        if pend_tp2[0] is not None:
            pend_tp2[0]()
            pend_tp2[0] = None

        rzz = [Res("zzA"), Res("zzB"), Res("zzC")]
        rwd = [Res(f"wd{i}") for i in range(8)]

        def wd_ap(kt):
            if kt < 32:
                return BB[:, kt * 1024:(kt + 1) * 1024]
            return SM[:, (kt - 32) * 1024:(kt - 31) * 1024]

        def wd_res(kt):
            return rwd[kt // 6]

        def f3_load_wd(nh, kts):
            for kt in kts:
                sl = kt % 4
                dma("SY" if kt % 2 == 0 else "AC", CFu_s(sl)[:, 0:1024],
                    wdT_d[kt * P:(kt + 1) * P,
                          nh * 1024:(nh + 1) * 1024],
                    f"cf{sl}", writes=[cfs[sl]])
                alias_pg = bbp[kt // 2] if kt < 32 else smp[(kt - 32) // 2]
                if kt % 2 == 1:
                    tr.emit("DV", lambda e, _kt=kt, _sl=sl: e.tensor_scalar(
                        wd_ap(_kt), CFu_s(_sl)[:, 0:1024], 128.0, None,
                        ALU.subtract),
                        reads=[cfs[sl]], writes=[wd_res(kt), alias_pg])
                else:
                    tr.emit("AC", lambda e, _kt=kt, _sl=sl: e.add(
                        wd_ap(_kt), CFu_s(_sl)[:, 0:1024], m128_t),
                        reads=[cfs[sl], st_res("m128")],
                        writes=[wd_res(kt), alias_pg])

        NBLK = HID // 512

        def f2_load_block(b):
            bpar = b % 2
            for (wsrc, base, slb) in ((wgT_d, bpar * 8, 0),
                                      (wuT_d, bpar * 8 + 4, 2)):
                for q in range(4):
                    sl = slb + q % 2
                    dma("SY", CFu_s(sl).rearrange("p (k c) -> p k c", c=512),
                        wsrc.rearrange("(k p) f -> p k f", p=P)[
                            :, 4 * q:4 * q + 4, b * 512:(b + 1) * 512],
                        f"cf{sl}", writes=[cfs[sl]])
                    tr.emit("DV", lambda e, _q=q, _sl=sl, _base=base:
                            e.tensor_scalar(BBpg(_base + _q), CFu_s(_sl),
                                            128.0, None, ALU.subtract),
                            reads=[cfs[sl]], writes=[bbp[base + q]])

        f2_load_block(0)
        for b in range(NBLK):
            bpar = b % 2
            wg_pgs = bbp[bpar * 8: bpar * 8 + 4]
            wu_pgs = bbp[bpar * 8 + 4: bpar * 8 + 8]
            for htl in range(4):
                if htl == 1 and b + 1 < NBLK:
                    f2_load_block(b + 1)
                if htl == 1 and b == NBLK - 1:
                    f3_load_wd(0, range(16, 32))
                ht = b * 4 + htl
                hpar = htl % 2
                g_pg, u_pg, z_pg = smp[hpar * 3], smp[hpar * 3 + 1], \
                    smp[hpar * 3 + 2]
                g_ap, u_ap, z_ap = SMpg(hpar * 3), SMpg(hpar * 3 + 1), \
                    SMpg(hpar * 3 + 2)

                def gmm(e, _base=bpar * 8, _htl=htl):
                    for k in range(NKE):
                        for n in range(4):
                            ins = e.matmul(
                                ps0[:, n * 512:(n + 1) * 512],
                                BB[:, (_base + k // 4) * 2048
                                   + (k % 4) * 512 + _htl * P:
                                   (_base + k // 4) * 2048
                                   + (k % 4) * 512 + _htl * P + P],
                                h1T(k, n * 512, 512),
                                start=(k == 0), stop=(k == NKE - 1))
                    return ins
                tr.emit("PE", gmm, reads=wg_pgs + [rAT], writes=bank[0:4])
                tr.emit("AC", lambda e, _ht=ht, _g=g_ap: e.activation(
                    _g, ps0[:], AF.Silu, bias=0.0,
                    scale=sgv_t[:, _ht:_ht + 1]),
                    reads=bank[0:4] + [st_res("sgv")], writes=[g_pg])

                def umm(e, _base=bpar * 8 + 4, _htl=htl):
                    for k in range(NKE):
                        for n in range(4):
                            ins = e.matmul(
                                ps1[:, n * 512:(n + 1) * 512],
                                BB[:, (_base + k // 4) * 2048
                                   + (k % 4) * 512 + _htl * P:
                                   (_base + k // 4) * 2048
                                   + (k % 4) * 512 + _htl * P + P],
                                h1T(k, n * 512, 512),
                                start=(k == 0), stop=(k == NKE - 1))
                    return ins
                tr.emit("PE", umm, reads=wu_pgs + [rAT], writes=bank[4:8])
                tr.emit("AC", lambda e, _ht=ht, _u=u_ap: e.activation(
                    _u, ps1[:], AF.Copy, bias=0.0,
                    scale=suv_t[:, _ht:_ht + 1]),
                    reads=bank[4:8] + [st_res("suv")], writes=[u_pg])
                tr.emit("DV", lambda e, _g=g_ap, _u=u_ap, _z=z_ap:
                        e.tensor_tensor(_z, _g, _u, ALU.mult),
                        reads=[g_pg, u_pg], writes=[z_pg])
                dma("GP",
                    zT_d[:, :, ht, :].transpose([1, 0, 2]),
                    z_ap.rearrange("p (t c) -> p t c", c=P),
                    "zTw", reads=[z_pg], writes=[rzT])


        y_half = SMf[:, 6 * 1024:7 * 1024]
        sdb_half = SMf[:, 7 * 1024:8 * 1024]
        junk_f3 = AT[:, 18432:19456]
        rjunk_f3 = Res("junkF3")
        ATf = AT.bitcast(F32)
        y_tmp = ATf[:, 9728:10752]
        rytmp = Res("ytmpF3")

        outsd = []
        for nh in range(2):
            dma("SY", sdb_half,
                sd_d[:, nh * 1024:(nh + 1) * 1024].partition_broadcast(P),
                "sdb", writes=[smp[7]])
            kts = ([kt for kt in range(NHT) if not (16 <= kt < 32)]
                   if nh == 0 else range(NHT))
            f3_load_wd(nh, kts)
            if DEBUG_DUMPS:
                for kt in range(NHT):
                    dma("GP", wdbg_d[nh, kt], wd_ap(kt), "dbg",
                        reads=[wd_res(kt)], writes=[st_res(f"wdbg{nh}_{kt}")])
            for st in range(NST):
                zsl = st % 3
                zz_ap = AT[:, zsl * 6144: zsl * 6144 + NHT * P]
                zw = [rzz[zsl]] + ([rAT] if (nh == 0 and st < 3) else [])
                dma("SY", zz_ap.rearrange("p (k c) -> p k c", c=P),
                    zT_d[st], f"zz{zsl}", reads=[rzT], writes=zw)
                if DEBUG_DUMPS and nh == 0 and st < 2:
                    dma("GP", zdbg_d[st], zz_ap, "dbg", reads=[rzz[zsl]],
                        writes=[st_res(f"zdbg{st}")])

                pb = bank[0:2] if st % 2 == 0 else bank[2:4]
                pso = ps0[:, 0:1024] if st % 2 == 0 else ps0[:, 1024:2048]

                def dmm(e, _zz=zz_ap, _pso=pso):
                    for kt in range(NHT):
                        for n in range(2):
                            ins = e.matmul(
                                _pso[:, n * 512:(n + 1) * 512],
                                _zz[:, kt * P:(kt + 1) * P],
                                wd_ap(kt)[:, n * 512:(n + 1) * 512],
                                start=(kt == 0), stop=(kt == NHT - 1))
                    return ins
                tr.emit("PE", dmm, reads=[rzz[zsl]] + rwd, writes=pb)
                a_s1 = stcol_of(f"F3a_{st}_{nh}")
                a_s2 = stcol_of(f"F3b_{st}_{nh}")
                rs1 = st_res(f"F3a_{st}_{nh}")
                rs2 = st_res(f"F3b_{st}_{nh}")
                yw0 = [rytmp] + ([rAT] if (nh == 0 and st == 0) else [])
                tr.emit("AC", lambda e, _pso=pso: e.copy(y_tmp, _pso),
                        reads=pb, writes=yw0)
                tr.emit("DV", lambda e, _a=a_s1:
                        e.scalar_tensor_tensor(
                            y_half, y_tmp, 1.0, sdb_half, ALU.mult, ALU.mult,
                            accum_out=_a),
                        reads=[rytmp, smp[7]], writes=[rs1, smp[6]])
                jw = [rs2, rjunk_f3] + ([rAT] if (nh == 0 and st == 0)
                                        else [])
                tr.emit("DV", lambda e, _a=a_s2: e.scalar_tensor_tensor(
                    junk_f3, y_half, 1.0, y_half, ALU.mult, ALU.mult,
                    accum_out=_a),
                    reads=[smp[6]], writes=jw)
                dma("GP", y_d[st * P:(st + 1) * P,
                              nh * 1024:(nh + 1) * 1024], y_half,
                    f"yw{st}", reads=[smp[6]], writes=[ry[st][nh]])

                if nh == 1:
                    ysl = st % 2
                    xsl = 2 + st % 2
                    dma("AC", CFs(ysl), y_d[st * P:(st + 1) * P, :],
                        f"cf{ysl}", reads=[ry[st][0], ry[st][1]],
                        writes=[cfs[ysl]])
                    dma("SY", CFs(xsl), x1_d[st * P:(st + 1) * P, :],
                        f"cf{xsl}", reads=[rx1[st]], writes=[cfs[xsl]])
                    a_t1 = stcol_of("F4s1"); a_t2 = stcol_of("F4s2")
                    rt1 = st_res("F4s1"); rt2 = st_res("F4s2")
                    tr.emit("DV", lambda e, _st=st: e.tensor_tensor(
                        a_t1, stcol_of(f"F3a_{_st}_0"),
                        stcol_of(f"F3a_{_st}_1"), ALU.add),
                        reads=[st_res(f"F3a_{st}_0"), st_res(f"F3a_{st}_1")],
                        writes=[rt1])
                    tr.emit("DV", lambda e, _st=st: e.tensor_tensor(
                        a_t2, stcol_of(f"F3b_{_st}_0"),
                        stcol_of(f"F3b_{_st}_1"), ALU.add),
                        reads=[st_res(f"F3b_{st}_0"), st_res(f"F3b_{st}_1")],
                        writes=[rt2])
                    a_m = stcol_of("F4m"); a_mq = stcol_of("F4mq")
                    a_va = stcol_of("F4va"); a_lv = stcol_of("F4lv")
                    a_rs = stcol_of("F4rs"); a_nm = stcol_of("F4nm")
                    rm = st_res("F4m"); rmq = st_res("F4mq")
                    rva = st_res("F4va"); rlv = st_res("F4lv")
                    rrs = st_res("F4rs"); rnm = st_res("F4nm")
                    tr.emit("DV", lambda e: e.tensor_scalar(
                        a_m, a_t1, 1.0 / E, None, ALU.mult),
                        reads=[rt1], writes=[rm])
                    tr.emit("DV", lambda e: e.scalar_tensor_tensor(
                        a_mq, a_m, -1.0, a_m, ALU.mult, ALU.mult),
                        reads=[rm], writes=[rmq])
                    tr.emit("DV", lambda e: e.scalar_tensor_tensor(
                        a_va, a_t2, 1.0 / E, a_mq, ALU.mult, ALU.add),
                        reads=[rt2, rmq], writes=[rva])
                    tr.emit("AC", lambda e: e.activation(
                        a_lv, a_va, AF.Ln, bias=eps_t, scale=1.0),
                        reads=[rva, st_res("eps")], writes=[rlv])
                    tr.emit("AC", lambda e: e.activation(
                        a_rs, a_lv, AF.Exp, bias=0.0, scale=-0.5),
                        reads=[rlv], writes=[rrs])
                    tr.emit("DV", lambda e: e.scalar_tensor_tensor(
                        a_nm, a_m, -1.0, a_rs, ALU.mult, ALU.mult),
                        reads=[rm, rrs], writes=[rnm])
                    tr.emit("DV", lambda e, _ysl=ysl: e.tensor_scalar(
                        CFs(_ysl), CFs(_ysl), a_rs, a_nm, ALU.mult, ALU.add),
                        reads=[rrs, rnm, cfs[ysl]], writes=[cfs[ysl]])
                    tr.emit("DV", lambda e, _ysl=ysl, _xsl=xsl:
                            e.tensor_tensor(CFs(_ysl), CFs(_ysl), CFs(_xsl),
                                            ALU.add),
                            reads=[cfs[ysl], cfs[xsl]], writes=[cfs[ysl]])
                    d = dma("GP", out_d[st * P:(st + 1) * P, :], CFs(ysl),
                            "outw", reads=[cfs[ysl]],
                            writes=[st_res(f"out{st}")])
                    outsd.append(d)

        tr.raw_wait("GP", outsd[-1])
        if DEBUG_SCRATCH:
            pass

        for en in tr.e.values():
            en.sem_c = E_(nc.semaphore(f"sem_{en.name}"))
        for sname, sd in tr.streams.items():
            sd[0] = E_(nc.semaphore(f"semd_{sname}"))

        with nc.Block() as block:
            @block.sync
            def _(eng):
                for t in tr.e["SY"].thunks:
                    t(eng)

            @block.gpsimd
            def _(eng):
                for t in tr.e["GP"].thunks:
                    t(eng)

            @block.vector
            def _(eng):
                for t in tr.e["DV"].thunks:
                    t(eng)

            @block.scalar
            def _(eng):
                for t in tr.e["AC"].thunks:
                    t(eng)

            @block.tensor
            def _(eng):
                for t in tr.e["PE"].thunks:
                    t(eng)

    return nc


_PROG_CACHE = {}


def _get_program():
    if "nc" not in _PROG_CACHE:
        _PROG_CACHE["nc"] = build_program()
    return _PROG_CACHE["nc"]


def kernel(L0_attn_x, L0_attn_w_qkv, L0_attn_w_out, L0_ffn_wq_gate,
           L0_ffn_s_gate, L0_ffn_wq_up, L0_ffn_s_up, L0_ffn_wq_down,
           L0_ffn_s_down):
    x = np.ascontiguousarray(np.asarray(L0_attn_x, dtype=np.float32))
    wqkvT = np.ascontiguousarray(np.asarray(L0_attn_w_qkv, np.float32).T)
    woutT = np.ascontiguousarray(np.asarray(L0_attn_w_out, np.float32).T)
    wgT = np.ascontiguousarray(np.asarray(L0_ffn_wq_gate, np.int32).T
                               .astype(np.uint8))
    wuT = np.ascontiguousarray(np.asarray(L0_ffn_wq_up, np.int32).T
                               .astype(np.uint8))
    wdT = np.ascontiguousarray(np.asarray(L0_ffn_wq_down, np.int32).T
                               .astype(np.uint8))
    sgv = np.ascontiguousarray(
        np.asarray(L0_ffn_s_gate, np.float32).reshape(NHT, P).T)
    suv = np.ascontiguousarray(
        np.asarray(L0_ffn_s_up, np.float32).reshape(NHT, P).T)
    sd = np.ascontiguousarray(
        np.asarray(L0_ffn_s_down, np.float32).reshape(1, E))

    ii = np.arange(C, dtype=np.int64)[:, None]
    jj = np.arange(2 * C, dtype=np.int64)[None, :]
    allowed = (jj >= ii) & (jj <= ii + C)
    maskq = np.where(allowed, np.float32(0.0),
                     np.float32(MASKVAL)).astype(np.float32)
    ident = np.eye(P, dtype=np.float32)

    nc = _get_program()
    shared = dict(wqkvT=wqkvT, woutT=woutT, wgT=wgT, wuT=wuT, wdT=wdT,
                  sgv=sgv, suv=suv, sd=sd, maskq=maskq, ident=ident)
    in_maps = [dict(x0=np.ascontiguousarray(x[c]), **shared)
               for c in range(8)]
    res = run_bass_kernel_spmd(nc, in_maps, core_ids=list(range(8)))
    return np.stack([r["out"] for r in res.results], axis=0)


# version 26
# speedup vs baseline: 1.1621x; 1.0011x over previous
import contextlib
import numpy as np
import concourse.bass as bass
import concourse.mybir as mybir
from concourse.bass_utils import run_bass_kernel_spmd

F32 = mybir.dt.float32
BF16 = mybir.dt.bfloat16
I32 = mybir.dt.int32
U8 = mybir.dt.uint8
AF = mybir.ActivationFunctionType
ALU = mybir.AluOpType
AX = mybir.AxisListType

P = 128
S = 2048
E = 2048
NH = 16
D = 128
C = 512
NCH = S // C
HID = 5632
NKE = E // P
NST = S // P
NHT = HID // P
EPS = 1e-5
SCALE = float(np.float64(1.0) / np.float64(np.sqrt(np.float64(D))))
MASKVAL = -3.0e9

DEBUG_SCRATCH = False
DEBUG_DUMPS = False
BISECT_IMMEDIATE_FLUSH = False
BISECT_NO_A0_DRAIN = False
BISECT_A0_END = False


class _Eng:
    def __init__(self, name):
        self.name = name
        self.thunks = []
        self.n_c = 0
        self.waited = {}
        self.sem_c = None


class Res:
    __slots__ = ("name", "w", "r")

    def __init__(self, name):
        self.name = name
        self.w = None
        self.r = {}


class TR:

    def __init__(self):
        self.e = {n: _Eng(n) for n in ("PE", "DV", "AC", "GP", "SY")}
        self.streams = {}

    def stream(self, name):
        if name not in self.streams:
            self.streams[name] = [None, 0]
        return self.streams[name]

    def emit(self, eng, fn, reads=(), writes=(), dma=None):
        st = self.e[eng]
        waits = []

        def need(dep):
            if dep is None:
                return
            semkey, val = dep
            if semkey[0] == "D":
                val = self.streams[semkey[1]][1]
            got = st.waited.get(semkey, 0)
            if got >= val:
                return
            st.waited[semkey] = val
            waits.append((semkey, val))

        for r in reads:
            need(r.w)
        for w in writes:
            need(w.w)
            for dep in w.r.values():
                need(dep)

        if dma is not None:
            key = f"{eng}:{dma}"
            sd = self.stream(key)
            sd[1] += 16
            dep = (("D", key), sd[1])
            inc_amt = 16
        else:
            st.n_c += 1
            dep = (("C", eng), st.n_c)
            inc_amt = 1

        def thunk(engobj, _waits=tuple(waits), _fn=fn, _dep=dep,
                  _inc=inc_amt, _tr=self):
            for (semkey, val) in _waits:
                engobj.wait_ge(_tr._sem(semkey), val)
            last = _fn(engobj)
            last.then_inc(_tr._sem(_dep[0]), _inc)

        st.thunks.append(thunk)

        for r in reads:
            r.r[eng] = dep
        for w in writes:
            w.w = dep
            w.r = {}
        return dep

    def _sem(self, semkey):
        if semkey[0] == "D":
            return self.streams[semkey[1]][0]
        return self.e[semkey[1]].sem_c

    def raw_wait(self, eng, dep):
        st = self.e[eng]
        semkey, val = dep

        def thunk(engobj, _semkey=semkey, _val=val, _tr=self):
            engobj.wait_ge(_tr._sem(_semkey), _val)

        st.thunks.append(thunk)


def build_program():
    nc = bass.Bass("TRN2", target_bir_lowering=False, debug=False)

    x0_d = nc.dram_tensor("x0", [S, E], F32, kind="ExternalInput").ap()
    wqkvT_d = nc.dram_tensor("wqkvT", [E, 3 * E], F32, kind="ExternalInput").ap()
    woutT_d = nc.dram_tensor("woutT", [E, E], F32, kind="ExternalInput").ap()
    wgT_d = nc.dram_tensor("wgT", [E, HID], U8, kind="ExternalInput").ap()
    wuT_d = nc.dram_tensor("wuT", [E, HID], U8, kind="ExternalInput").ap()
    wdT_d = nc.dram_tensor("wdT", [HID, E], U8, kind="ExternalInput").ap()
    sgv_d = nc.dram_tensor("sgv", [P, NHT], F32, kind="ExternalInput").ap()
    suv_d = nc.dram_tensor("suv", [P, NHT], F32, kind="ExternalInput").ap()
    sd_d = nc.dram_tensor("sd", [1, E], F32, kind="ExternalInput").ap()
    maskq_d = nc.dram_tensor("maskq", [C, 2 * C], F32, kind="ExternalInput").ap()
    ident_d = nc.dram_tensor("ident", [P, P], F32, kind="ExternalInput").ap()
    out_d = nc.dram_tensor("out", [S, E], F32, kind="ExternalOutput").ap()

    scratch_kind = "ExternalOutput" if DEBUG_SCRATCH else "Internal"
    x1_d = nc.dram_tensor("x1s", [S, E], F32, kind=scratch_kind).ap()
    atT_d = nc.dram_tensor("atTs", [NST, P, NH, P], BF16, kind=scratch_kind).ap()
    zT_d = nc.dram_tensor("zTs", [NST, P, NHT, P], BF16, kind=scratch_kind).ap()
    y_d = nc.dram_tensor("ys", [S, E], F32, kind=scratch_kind).ap()
    rec_d = nc.dram_tensor("recs", [NST, E], F32, kind="Internal").ap()
    if DEBUG_SCRATCH:
        wdbg_d = nc.dram_tensor("wdbg", [2, NHT, P, 1024], BF16,
                                kind="ExternalOutput").ap()
        zdbg_d = nc.dram_tensor("zdbg", [2, P, NHT * P], BF16,
                                kind="ExternalOutput").ap()
        h2dbg_d = nc.dram_tensor("h2dbg", [P, 16 * 2048], BF16,
                                 kind="ExternalOutput").ap()

    tr = TR()
    ctx = contextlib.ExitStack()

    with ctx:
        E_ = ctx.enter_context
        AT = E_(nc.sbuf_tensor("AT", [P, 16 * 2048], BF16))
        BB = E_(nc.sbuf_tensor("BB", [P, 16 * 2048], BF16))
        SM = E_(nc.sbuf_tensor("SM", [P, 8 * 2048], BF16))
        CF = E_(nc.sbuf_tensor("CF", [P, 4 * 2048], F32))
        MK = E_(nc.sbuf_tensor("MK", [P, 4 * 1024], BF16))
        IDB = E_(nc.sbuf_tensor("IDB", [P, P], BF16))
        IDF = E_(nc.sbuf_tensor("IDF", [P, P], F32))
        AB = E_(nc.sbuf_tensor("AB", [P, 1024], BF16))
        STT = E_(nc.sbuf_tensor("STT", [P, 256], F32))
        DEN = E_(nc.sbuf_tensor("DEN", [P, 256], F32))
        DENR = E_(nc.sbuf_tensor("DENR", [P, 256], F32))
        RCS = E_(nc.sbuf_tensor("RCS", [P, P], F32))

        ps0 = E_(nc.psum_tensor("ps0", [P, 2048], F32))
        ps1 = E_(nc.psum_tensor("ps1", [P, 2048], F32))
        ps1b = ps1.bitcast(BF16)

        bank = [Res(f"bank{i}") for i in range(8)]
        rAT = Res("AT")
        rATc = [Res(f"ATc{i}") for i in range(NST)]
        bbp = [Res(f"BB{i}") for i in range(16)]
        smp = [Res(f"SM{i}") for i in range(8)]
        cfs = [Res(f"CF{i}") for i in range(4)]
        rMK = Res("MK")
        rIDB = Res("IDB")
        rIDF = Res("IDF")
        rABs = [Res("AB0"), Res("AB1")]
        stc = {}

        def st_res(name):
            if name not in stc:
                stc[name] = Res(name)
            return stc[name]

        rx1 = [Res(f"x1_{i}") for i in range(NST)]
        rDEN = Res("DEN")
        rDENR = Res("DENR")
        rRCS = Res("RCS")
        rREC = [Res(f"RECscr{i}") for i in range(NST)]
        ratT = Res("atT")
        rzT = Res("zT")
        ry = [[Res(f"y_{i}_{h}") for h in range(2)] for i in range(NST)]

        _stcol = [0]

        def stcol(n=1):
            c = _stcol[0]
            _stcol[0] += n
            assert _stcol[0] <= 256
            return STT[:, c:c + n]

        sgv_t = stcol(NHT)
        suv_t = stcol(NHT)
        eps_t = stcol(1)
        m128_t = stcol(1)

        def BBpg(i, n=1):
            return BB[:, i * 2048:(i + n) * 2048]

        def SMpg(i, n=1):
            return SM[:, i * 2048:(i + n) * 2048]

        SMf = SM.bitcast(F32)

        def SMf_pg(i, n=1):
            return SMf[:, i * 1024:(i + n) * 1024]

        def CFs(i, n=1):
            return CF[:, i * 2048:(i + n) * 2048]

        CFi = CF.bitcast(I32)

        def CFi_s(i, n=1):
            return CFi[:, i * 2048:(i + n) * 2048]

        CFu = CF.bitcast(U8)

        def CFu_s(i):
            return CFu[:, i * 8192:i * 8192 + 2048]

        def dma(eng, dst, src, stream, reads=(), writes=()):
            return tr.emit(eng, lambda e, _d=dst, _s=src: e.dma_start(_d, _s),
                           reads=reads, writes=writes, dma=stream)

        def ln_stats(pref, src_ap, width, src_reads, junk_ap, junk_res,
                     extra_sum=None):
            s1 = st_res(pref + "s1"); s2 = st_res(pref + "s2")
            m = st_res(pref + "m"); mq = st_res(pref + "mq")
            va = st_res(pref + "va"); lv = st_res(pref + "lv")
            rs = st_res(pref + "rs"); nm = st_res(pref + "nm")
            a_s1 = stcol_of(pref + "s1"); a_s2 = stcol_of(pref + "s2")
            a_m = stcol_of(pref + "m"); a_mq = stcol_of(pref + "mq")
            a_va = stcol_of(pref + "va"); a_lv = stcol_of(pref + "lv")
            a_rs = stcol_of(pref + "rs"); a_nm = stcol_of(pref + "nm")

            tr.emit("DV", lambda e: e.reduce_sum(a_s1, src_ap, axis=AX.X),
                    reads=src_reads, writes=[s1])
            tr.emit("DV", lambda e: e.scalar_tensor_tensor(
                junk_ap, src_ap, 1.0, src_ap, ALU.mult, ALU.mult,
                accum_out=a_s2), reads=src_reads, writes=[s2, junk_res])
            tr.emit("DV", lambda e: e.tensor_scalar(
                a_m, a_s1, 1.0 / width, None, ALU.mult),
                reads=[s1], writes=[m])
            tr.emit("DV", lambda e: e.scalar_tensor_tensor(
                a_mq, a_m, -1.0, a_m, ALU.mult, ALU.mult),
                reads=[m], writes=[mq])
            tr.emit("DV", lambda e: e.scalar_tensor_tensor(
                a_va, a_s2, 1.0 / width, a_mq, ALU.mult, ALU.add),
                reads=[s2, mq], writes=[va])
            tr.emit("AC", lambda e: e.activation(
                a_lv, a_va, AF.Ln, bias=eps_t, scale=1.0),
                reads=[va, st_res("eps")], writes=[lv])
            tr.emit("AC", lambda e: e.activation(
                a_rs, a_lv, AF.Exp, bias=0.0, scale=-0.5),
                reads=[lv], writes=[rs])
            tr.emit("DV", lambda e: e.scalar_tensor_tensor(
                a_nm, a_m, -1.0, a_rs, ALU.mult, ALU.mult),
                reads=[m, rs], writes=[nm])
            return a_rs, a_nm, rs, nm

        _st_aps = {}

        def stcol_of(name, n=1):
            if name not in _st_aps:
                _st_aps[name] = stcol(n)
            return _st_aps[name]

        tr.emit("DV", lambda e: e.memset(eps_t, EPS), writes=[st_res("eps")])
        tr.emit("DV", lambda e: e.memset(m128_t, -128.0),
                writes=[st_res("m128")])
        dma("SY", sgv_t, sgv_d, "sg", writes=[st_res("sgv")])
        dma("SY", suv_t, suv_d, "su", writes=[st_res("suv")])
        dma("SY", CFs(0)[:, 0:P], ident_d, "cf0", writes=[cfs[0]])
        tr.emit("DV", lambda e: e.tensor_copy(IDB[:], CFs(0)[:, 0:P]),
                reads=[cfs[0]], writes=[rIDB])
        tr.emit("DV", lambda e: e.tensor_copy(IDF[:], CFs(0)[:, 0:P]),
                reads=[cfs[0]], writes=[rIDF])
        dma("SY", CFs(2, 2).rearrange("p (t c) -> p t c", c=1024),
            maskq_d.rearrange("(t p) c -> p t c", p=P),
            "cf2", writes=[cfs[2], cfs[3]])
        tr.emit("DV", lambda e: e.tensor_copy(MK[:], CFs(2, 2)),
                reads=[cfs[2], cfs[3]], writes=[rMK])

        def mask_tile(t):
            return MK[:, t * 1024:(t + 1) * 1024]

        pe_fill = []

        def drain_fill(n):
            for _ in range(min(n, len(pe_fill))):
                pe_fill.pop(0)()

        def stage_v_group(g, split=False):
            if not split:
                dma("SY", CF[:].rearrange("p (k c) -> p k c", c=512),
                    wqkvT_d.rearrange("(k p) f -> p k f", p=P)[
                        :, :, 2 * E + g * 512:2 * E + (g + 1) * 512],
                    "cf0", writes=cfs)
                tr.emit("DV", lambda e: e.tensor_copy(BBpg(4, 4), CF[:]),
                        reads=cfs, writes=[bbp[4], bbp[5], bbp[6], bbp[7]])
                return
            for qtr in range(4):
                sl = 2 + qtr % 2
                dma("SY", CFs(sl).rearrange("p (k c) -> p k c", c=512),
                    wqkvT_d.rearrange("(k p) f -> p k f", p=P)[
                        :, 4 * qtr:4 * qtr + 4,
                        2 * E + g * 512:2 * E + (g + 1) * 512],
                    f"cf{sl}", writes=[cfs[sl]])
                tr.emit("DV", lambda e, _q=qtr, _sl=sl: e.tensor_copy(
                    BBpg(4 + _q), CFs(_sl)),
                    reads=[cfs[sl]], writes=[bbp[4 + qtr]])

        def enq_v_mms():
            for q4 in range(4):
                def unit(_q4=q4):
                    for vst in range(_q4 * 4, _q4 * 4 + 4):
                        vb = bank[4 + vst % 2]
                        pso = ps1[:, (vst % 2) * 512:(vst % 2) * 512 + 512]

                        def vmm(e, _vst=vst, _pso=pso):
                            for k in range(NKE):
                                ins = e.matmul(
                                    _pso,
                                    AT[:, k * 2048 + _vst * P:
                                       k * 2048 + (_vst + 1) * P],
                                    BB[:, 4 * 2048 + k * 512:
                                       4 * 2048 + (k + 1) * 512],
                                    start=(k == 0), stop=(k == NKE - 1))
                            return ins
                        tr.emit("PE", vmm,
                                reads=[rATc[vst],
                                       bbp[4], bbp[5], bbp[6], bbp[7]],
                                writes=[vb])
                        tr.emit("AC", lambda e, _vst=vst, _pso=pso: e.copy(
                            BB[:, (_vst // 4) * 2048 + (_vst % 4) * 512:
                               (_vst // 4) * 2048 + (_vst % 4) * 512 + 512],
                            _pso), reads=[vb], writes=[bbp[vst // 4]])
                pe_fill.append(unit)

        def stage_qk(hh, slots=None):
            par = hh % 2
            if slots is None:
                slots = (par * 2, par * 2 + 1)
            for (slot, fcol, wpg, wap) in (
                    (slots[0], hh * P, bbp[8 + 2 * par], BBpg(8 + 2 * par)),
                    (slots[1], E + hh * P, bbp[9 + 2 * par],
                     BBpg(9 + 2 * par))):
                stg = cfs[slot]
                stg_ap = CFs(slot)
                dma("SY", stg_ap.rearrange("p (k c) -> p k c", c=P),
                    wqkvT_d.rearrange("(k p) f -> p k f", p=P)[
                        :, :, fcol:fcol + P],
                    f"cf{slot}", writes=[stg])
                tr.emit("DV", lambda e, _w=wap, _s=stg_ap:
                        e.tensor_copy(_w, _s), reads=[stg], writes=[wpg])

        def _qk_subunit(wpg, wap, opg, oap, half, n, isq):
            win = [rATc[i] for i in range(half * 8 + n * 4,
                                          half * 8 + n * 4 + 4)]

            def qkmm(e, _w=wap, _h2=half, _nn=n):
                for k in range(NKE):
                    ins = e.matmul(
                        ps1[:, _nn * 512:(_nn + 1) * 512],
                        _w[:, k * P:(k + 1) * P],
                        h1T(k, _h2 * 1024 + _nn * 512, 512),
                        start=(k == 0), stop=(k == NKE - 1))
                return ins
            tr.emit("PE", qkmm, reads=[wpg] + win, writes=[bank[4 + n]])
            if n == 1:
                sc = SCALE if isq else 1.0
                tr.emit("AC", lambda e, _o=oap, _h2=half, _sc=sc:
                        e.activation(
                            _o[:, _h2 * 1024:(_h2 + 1) * 1024],
                            ps1[:, 0:1024], AF.Copy, bias=0.0, scale=_sc),
                        reads=[bank[4], bank[5]], writes=[opg])

        def enq_qk_mms(hh, combined=False):
            par = hh % 2
            for (isq, wpg, wap, opg, oap) in (
                    (True, bbp[8 + 2 * par], BBpg(8 + 2 * par),
                     bbp[12 + 2 * par], BBpg(12 + 2 * par)),
                    (False, bbp[9 + 2 * par], BBpg(9 + 2 * par),
                     bbp[13 + 2 * par], BBpg(13 + 2 * par))):
                for half in range(2):
                    if combined:
                        def unit(_a=(wpg, wap, opg, oap, half, isq)):
                            w1, w2, o1, o2, hf, q = _a
                            _qk_subunit(w1, w2, o1, o2, hf, 0, q)
                            _qk_subunit(w1, w2, o1, o2, hf, 1, q)
                        pe_fill.append(unit)
                    else:
                        for n in range(2):
                            def unit(_a=(wpg, wap, opg, oap, half, n, isq)):
                                _qk_subunit(*_a)
                            pe_fill.append(unit)


        stage_v_group(0, split=not BISECT_NO_A0_DRAIN)
        stage_qk(0, slots=(2, 3))
        _q_units = []
        enq_v_mms()
        _v_units = pe_fill[:]
        del pe_fill[:]
        enq_qk_mms(0, combined=True)
        _qk0_units = pe_fill[:]
        del pe_fill[:]
        pe_fill.extend([_v_units[0], _v_units[1], _qk0_units[0],
                        _qk0_units[2], _v_units[2], _v_units[3],
                        _qk0_units[1], _qk0_units[3]])
        _a0_drains = {3: 1, 7: 3, 11: 1, 15: 3}
        if BISECT_NO_A0_DRAIN:
            _a0_drains = {}
        if BISECT_A0_END:
            _a0_drains = {15: 8}
        for st in range(NST):
            sl = st % 2
            dma("GP", CFs(sl), x0_d[st * P:(st + 1) * P, :], f"cf{sl}",
                writes=[cfs[sl]])
            rstd, negmr, r_rs, r_nm = ln_stats(
                "A0", CFs(sl), E, [cfs[sl]], SMpg(2), smp[2])
            hp = st % 2
            tr.emit("DV", lambda e, _sl=sl, _hp=hp, _rs=rstd, _nm=negmr:
                    e.tensor_scalar(SMpg(_hp), CFs(_sl), _rs, _nm,
                                    ALU.mult, ALU.add),
                    reads=[cfs[sl], r_rs, r_nm], writes=[smp[hp]])
            boff = 0 if st % 2 == 0 else 2048
            bks = [bank[4], bank[5]] if st % 2 == 0 else [bank[6], bank[7]]

            def tp_fn(e, _hp=hp, _boff=boff):
                for k in range(NKE):
                    ins = e.transpose(
                        ps1b[:, _boff + k * P:_boff + (k + 1) * P],
                        SMpg(_hp)[:, k * P:(k + 1) * P], IDB[:])
                return ins
            tr.emit("PE", tp_fn, reads=[smp[hp], rIDB], writes=bks)

            def ev_fn(e, _st=st, _boff=boff):
                dst = AT[:].rearrange("p (k c) -> p k c", c=2048)[
                    :, :, _st * P:(_st + 1) * P]
                src = ps1b[:, _boff:_boff + 2048].rearrange(
                    "p (k c) -> p k c", c=P)
                return e.copy(dst, src)
            tr.emit("AC", ev_fn, reads=bks, writes=[rATc[st], rAT])
            drain_fill(_a0_drains.get(st, 0))

        def h1T(k, lo, n):
            return AT[:, k * 2048 + lo:k * 2048 + lo + n]

        drain_fill(99)

        for h in range(NH):
            par = h % 2
            if h + 1 < NH:
                stage_qk(h + 1)
                enq_qk_mms(h + 1)

            def vtile(sk_t, hh):
                return BB[:, (sk_t // 4) * 2048 + (sk_t % 4) * 512
                          + (hh % 4) * P:
                          (sk_t // 4) * 2048 + (sk_t % 4) * 512
                          + (hh % 4) * P + P]

            qT_pg, kT_pg = bbp[12 + 2 * par], bbp[13 + 2 * par]
            pend_tp = [None]

            def flush_tp():
                if pend_tp[0] is not None:
                    pend_tp[0]()
                    pend_tp[0] = None

            for chn in range(NCH):
                nsk = C if chn == 0 else 2 * C
                ksk0 = (chn - 1) * C if chn > 0 else 0
                nskt = nsk // P
                cpar = chn % 2
                probs_pg = [smp[2 * cpar], smp[2 * cpar + 1]]
                probs_ap = SMpg(2 * cpar, 2)
                probsT_pg = [smp[4 + 2 * cpar], smp[5 + 2 * cpar]]
                probsT_ap = SMpg(4 + 2 * cpar, 2)

                for t in range(4):
                    sq0 = chn * C + t * P
                    reg = (t % 2) * 1024
                    sc_bks = [bank[2 * (t % 2)], bank[2 * (t % 2) + 1]]

                    def smm(e, _t=t, _sq0=sq0, _chn=chn, _par=par,
                            _nsk=nsk, _ksk0=ksk0, _reg=reg):
                        scp = ps0[:, _reg:_reg + _nsk]
                        mkap = (mask_tile(_t)[:, 512:1024] if _chn == 0
                                else mask_tile(_t))
                        for half in range(_nsk // 512):
                            e.matmul(
                                scp[:, half * 512:(half + 1) * 512],
                                BB[:, (12 + 2 * _par) * 2048 + _sq0:
                                   (12 + 2 * _par) * 2048 + _sq0 + P],
                                BB[:, (13 + 2 * _par) * 2048 + _ksk0
                                   + half * 512:
                                   (13 + 2 * _par) * 2048 + _ksk0
                                   + half * 512 + 512],
                                start=True, stop=False)
                            ins = e.matmul(
                                scp[:, half * 512:(half + 1) * 512],
                                IDB[:],
                                mkap[:, half * 512:(half + 1) * 512],
                                start=False, stop=True)
                        return ins
                    tr.emit("PE", smm,
                            reads=[qT_pg, kT_pg, rMK, rIDB], writes=sc_bks)

                    nbr = st_res(f"nb{par}_{cpar}_{t}")
                    a_nb = stcol_of(f"nb{par}_{cpar}_{t}")
                    den_ap = DEN[:, h * 16 + chn * 4 + t:
                                 h * 16 + chn * 4 + t + 1]

                    tr.emit("DV", lambda e, _nsk=nsk, _a=a_nb, _reg=reg:
                            e.reduce_max(_a, ps0[:, _reg:_reg + _nsk],
                                         axis=AX.X, negate=True),
                            reads=sc_bks, writes=[nbr])
                    pr_ap = probs_ap[:, t * nsk:t * nsk + nsk]
                    pgi = (t * nsk) // 2048
                    tr.emit("AC", lambda e, _nsk=nsk, _pr=pr_ap, _reg=reg,
                            _nb=a_nb, _dn=den_ap:
                            e.activation(_pr, ps0[:, _reg:_reg + _nsk],
                                         AF.Exp, bias=_nb, scale=1.0,
                                         accum_out=_dn),
                            reads=sc_bks + [nbr],
                            writes=[rDEN, probs_pg[pgi]])
                    if t == 1:
                        drain_fill(1)

                flush_tp()
                drain_fill(1)

                def mk_tp(_chn=chn, _nsk=nsk, _nskt=nskt, _par=par,
                          _h=h, _probs_ap=probs_ap,
                          _probs_pg=tuple(probs_pg),
                          _probsT_ap=probsT_ap,
                          _probsT_pg=tuple(probsT_pg)):
                    def run():
                        npair = _nskt // 2
                        pv_ps = ps1[:, 1536:2048]
                        pv_bk = bank[7]
                        for pair in range(npair):
                            ptb = 6 + pair % 2
                            pt_ps = ps1b[:, (ptb - 4) * 1024:
                                         (ptb - 3) * 1024]
                            pt_bk = bank[ptb]

                            def tpf(e, _pair=pair, _pt=pt_ps):
                                for t in range(4):
                                    for j in range(2):
                                        skt = _pair * 2 + j
                                        ins = e.transpose(
                                            _pt[:, j * 512 + t * P:
                                                j * 512 + (t + 1) * P],
                                            _probs_ap[:, t * _nsk + skt * P:
                                                      t * _nsk
                                                      + (skt + 1) * P],
                                            IDB[:])
                                return ins
                            tr.emit("PE", tpf,
                                    reads=[_probs_pg[0], _probs_pg[1],
                                           rIDB],
                                    writes=[pt_bk])
                            tr.emit("DV", lambda e, _pair=pair, _pt=pt_ps:
                                    e.tensor_copy(
                                        _probsT_ap[:, _pair * 1024:
                                                   (_pair + 1) * 1024],
                                        _pt),
                                    reads=[pt_bk],
                                    writes=[_probsT_pg[pair // 2]])

                        def pvf(e):
                            for skt in range(_nskt):
                                glob_skt = ((0 if _chn == 0
                                             else (_chn - 1) * 4) + skt)
                                ins = e.matmul(
                                    pv_ps, vtile(glob_skt, _h),
                                    _probsT_ap[:, (skt // 2) * 1024
                                               + (skt % 2) * 512:
                                               (skt // 2) * 1024
                                               + (skt % 2) * 512 + 512],
                                    start=(skt == 0),
                                    stop=(skt == _nskt - 1))
                            return ins
                        tr.emit("PE", pvf,
                                reads=[_probsT_pg[0], _probsT_pg[1],
                                       bbp[0], bbp[1], bbp[2], bbp[3]],
                                writes=[pv_bk])
                        ab = rABs[_chn % 2]
                        ab_ap = AB[:, (_chn % 2) * 512:
                                   (_chn % 2) * 512 + 512]
                        tr.emit("AC", lambda e, _ab=ab_ap:
                                e.copy(_ab, pv_ps), reads=[pv_bk],
                                writes=[ab])
                        dma("GP",
                            atT_d[_chn * 4:(_chn + 1) * 4, :, _h, :]
                            .transpose([1, 0, 2]),
                            ab_ap.rearrange("p (t c) -> p t c", c=P),
                            "atw", reads=[ab], writes=[ratT])
                    return run
                pend_tp[0] = mk_tp()
                if BISECT_IMMEDIATE_FLUSH:
                    flush_tp()

            flush_tp()
            drain_fill(1)

            if h + 1 < NH and (h + 1) % 4 == 0:
                stage_v_group((h + 1) // 4)
                enq_v_mms()
            drain_fill(2)
        drain_fill(99)

        for k in range(NKE):
            sl = 2 + (k % 2)
            dma("SY", CFs(sl), woutT_d[k * P:(k + 1) * P, :], f"cf{sl}",
                writes=[cfs[sl]])
            tr.emit("DV", lambda e, _k=k, _sl=sl:
                    e.tensor_copy(BBpg(_k), CFs(_sl)),
                    reads=[cfs[sl]], writes=[bbp[k]])

        y_sb = SMf_pg(0, 2)
        x1_sb = SMf_pg(2, 2)
        pend_tp2 = [None]

        tr.emit("DV", lambda e: e.reciprocal(DENR[:], DEN[:]),
                reads=[rDEN], writes=[rDENR])
        def rec_build(st2):
            base = (st2 // 4) * 4 + st2 % 4
            q = st2 % 4
            tp_ps = ps1[0:16, q * 128:q * 128 + 128]

            def rtp(e, _b=base, _p=tp_ps):
                return e.transpose(_p, DENR[:, _b::16], IDF[:])
            tr.emit("PE", rtp, reads=[rDENR, rIDF], writes=[bank[4]])
            tr.emit("AC", lambda e, _p=tp_ps: e.copy(RCS[0:16, :], _p),
                    reads=[bank[4]], writes=[rRCS])
            dma("GP", rec_d[st2].rearrange("(a c) -> a c", a=16),
                RCS[0:16, :], "recw", reads=[rRCS],
                writes=[rREC[st2]])

        def at_load(st2):
            asl2 = 4 + st2 % 2
            dma("AC", SMpg(asl2), atT_d[st2].rearrange("p h c -> p (h c)"),
                f"sm{asl2}", reads=[ratT], writes=[smp[asl2]])
            csl = 2 + st2 % 2
            dma("AC", CFs(csl),
                rec_d[st2].rearrange("(o c) -> o c", o=1)
                .partition_broadcast(P),
                f"cf{csl}", reads=[rREC[st2]], writes=[cfs[csl]])
            tr.emit("DV", lambda e, _a=SMpg(asl2), _c=CFs(csl):
                    e.tensor_tensor(_a, _a, _c, ALU.mult),
                    reads=[smp[asl2], cfs[csl]],
                    writes=[smp[asl2]])

        rec_build(0)
        rec_build(1)
        rec_build(2)
        at_load(0)
        at_load(1)
        for st in range(NST):
            asl = 4 + st % 2
            dma("GP", CFs(st % 2), x0_d[st * P:(st + 1) * P, :],
                f"cf{st % 2}", writes=[cfs[st % 2]])

            def projmm(e, _asl=asl):
                for n in range(4):
                    for k in range(NKE):
                        ins = e.matmul(
                            ps0[:, n * 512:(n + 1) * 512],
                            SM[:, _asl * 2048 + k * P:
                               _asl * 2048 + (k + 1) * P],
                            BB[:, k * 2048 + n * 512:k * 2048 + n * 512 + 512],
                            start=(k == 0), stop=(k == NKE - 1))
                return ins
            tr.emit("PE", projmm,
                    reads=[smp[asl]] + bbp[0:16], writes=bank[0:4])
            if st + 3 < NST:
                rec_build(st + 3)
            if st + 2 < NST:
                at_load(st + 2)
            if pend_tp2[0] is not None:
                pend_tp2[0]()
                pend_tp2[0] = None
            s1p = [st_res(f"A3p{i}") for i in range(4)]
            for n in range(4):
                a_p = stcol_of(f"A3p{n}")
                tr.emit("AC", lambda e, _n=n, _a=a_p: e.activation(
                    y_sb[:, _n * 512:(_n + 1) * 512],
                    ps0[:, _n * 512:(_n + 1) * 512],
                    AF.Copy, bias=0.0, scale=1.0, accum_out=_a),
                    reads=[bank[n]], writes=[s1p[n], smp[n // 2]])
            rs1 = st_res("A3s1")
            a_s1 = stcol_of("A3s1")
            tr.emit("DV", lambda e: e.tensor_tensor(
                a_s1, stcol_of("A3p0"), stcol_of("A3p1"), ALU.add),
                reads=[s1p[0], s1p[1]], writes=[rs1])
            tr.emit("DV", lambda e: e.tensor_tensor(
                a_s1, a_s1, stcol_of("A3p2"), ALU.add),
                reads=[rs1, s1p[2]], writes=[rs1])
            tr.emit("DV", lambda e: e.tensor_tensor(
                a_s1, a_s1, stcol_of("A3p3"), ALU.add),
                reads=[rs1, s1p[3]], writes=[rs1])
            rs2 = st_res("A3s2")
            a_s2 = stcol_of("A3s2")
            tr.emit("DV", lambda e: e.scalar_tensor_tensor(
                SMpg(7), y_sb, 1.0, y_sb, ALU.mult, ALU.mult,
                accum_out=a_s2),
                reads=[smp[0], smp[1]], writes=[rs2, smp[7]])
            a_m = stcol_of("A3m"); a_mq = stcol_of("A3mq")
            a_va = stcol_of("A3va"); a_lv = stcol_of("A3lv")
            a_rs = stcol_of("A3rs"); a_nm = stcol_of("A3nm")
            rm = st_res("A3m"); rmq = st_res("A3mq"); rva = st_res("A3va")
            rlv = st_res("A3lv"); rrs = st_res("A3rs"); rnm = st_res("A3nm")
            tr.emit("DV", lambda e: e.tensor_scalar(
                a_m, a_s1, 1.0 / E, None, ALU.mult), reads=[rs1], writes=[rm])
            tr.emit("DV", lambda e: e.scalar_tensor_tensor(
                a_mq, a_m, -1.0, a_m, ALU.mult, ALU.mult),
                reads=[rm], writes=[rmq])
            tr.emit("DV", lambda e: e.scalar_tensor_tensor(
                a_va, a_s2, 1.0 / E, a_mq, ALU.mult, ALU.add),
                reads=[rs2, rmq], writes=[rva])
            tr.emit("AC", lambda e: e.activation(
                a_lv, a_va, AF.Ln, bias=eps_t, scale=1.0),
                reads=[rva, st_res("eps")], writes=[rlv])
            tr.emit("AC", lambda e: e.activation(
                a_rs, a_lv, AF.Exp, bias=0.0, scale=-0.5),
                reads=[rlv], writes=[rrs])
            tr.emit("DV", lambda e: e.scalar_tensor_tensor(
                a_nm, a_m, -1.0, a_rs, ALU.mult, ALU.mult),
                reads=[rm, rrs], writes=[rnm])
            tr.emit("DV", lambda e: e.tensor_scalar(
                y_sb, y_sb, a_rs, a_nm, ALU.mult, ALU.add),
                reads=[rrs, rnm, smp[0], smp[1]], writes=[smp[0], smp[1]])
            tr.emit("DV", lambda e, _sl=st % 2: e.tensor_tensor(
                x1_sb, y_sb, CFs(_sl), ALU.add),
                reads=[smp[0], smp[1], cfs[st % 2]], writes=[smp[2], smp[3]])
            dma("GP", x1_d[st * P:(st + 1) * P, :], x1_sb,
                f"x1w{st}", reads=[smp[2], smp[3]], writes=[rx1[st]])
            rstd2, negmr2, r_rs2, r_nm2 = ln_stats(
                "A3b", x1_sb, E, [smp[2], smp[3]], SMpg(7), smp[7])
            tr.emit("DV", lambda e, _rs=rstd2, _nm=negmr2: e.tensor_scalar(
                SMpg(6), x1_sb, _rs, _nm, ALU.mult, ALU.add),
                reads=[smp[2], smp[3], r_rs2, r_nm2], writes=[smp[6]])
            boff = 0 if st % 2 == 0 else 2048
            bks = [bank[4], bank[5]] if st % 2 == 0 else [bank[6], bank[7]]

            def mk_pend(_st=st, _boff=boff, _bks=bks):
                def run():
                    def tp2(e):
                        for k in range(NKE):
                            ins = e.transpose(
                                ps1b[:, _boff + k * P:_boff + (k + 1) * P],
                                SMpg(6)[:, k * P:(k + 1) * P], IDB[:])
                        return ins
                    tr.emit("PE", tp2, reads=[smp[6], rIDB], writes=_bks)

                    def ev2(e):
                        dst = AT[:].rearrange("p (k c) -> p k c", c=2048)[
                            :, :, _st * P:(_st + 1) * P]
                        srcp = ps1b[:, _boff:_boff + 2048].rearrange(
                            "p (k c) -> p k c", c=P)
                        return e.copy(dst, srcp)
                    tr.emit("AC", ev2, reads=_bks,
                            writes=[rATc[_st], rAT])
                return run
            pend_tp2[0] = mk_pend()

        if pend_tp2[0] is not None:
            pend_tp2[0]()
            pend_tp2[0] = None
        if DEBUG_DUMPS:
            dma("GP", h2dbg_d, AT[:], "dbg", reads=[rAT],
                writes=[st_res("h2dbg")])

        rzz = [Res("zzA"), Res("zzB"), Res("zzC")]
        rwd = [Res(f"wd{i}") for i in range(8)]

        def wd_ap(kt):
            if kt < 32:
                return BB[:, kt * 1024:(kt + 1) * 1024]
            return SM[:, (kt - 32) * 1024:(kt - 31) * 1024]

        def wd_res(kt):
            return rwd[kt // 6]

        def f3_load_wd(nh, kts):
            for kt in kts:
                sl = kt % 4
                dma("SY" if kt % 2 == 0 else "AC", CFu_s(sl)[:, 0:1024],
                    wdT_d[kt * P:(kt + 1) * P,
                          nh * 1024:(nh + 1) * 1024],
                    f"cf{sl}", writes=[cfs[sl]])
                alias_pg = bbp[kt // 2] if kt < 32 else smp[(kt - 32) // 2]
                if kt % 2 == 1:
                    tr.emit("DV", lambda e, _kt=kt, _sl=sl: e.tensor_scalar(
                        wd_ap(_kt), CFu_s(_sl)[:, 0:1024], 128.0, None,
                        ALU.subtract),
                        reads=[cfs[sl]], writes=[wd_res(kt), alias_pg])
                else:
                    tr.emit("AC", lambda e, _kt=kt, _sl=sl: e.add(
                        wd_ap(_kt), CFu_s(_sl)[:, 0:1024], m128_t),
                        reads=[cfs[sl], st_res("m128")],
                        writes=[wd_res(kt), alias_pg])

        NBLK = HID // 512

        def f2_load_block(b):
            bpar = b % 2
            for (wsrc, base, slb) in ((wgT_d, bpar * 8, 0),
                                      (wuT_d, bpar * 8 + 4, 2)):
                for q in range(4):
                    sl = slb + q % 2
                    dma("SY", CFu_s(sl).rearrange("p (k c) -> p k c", c=512),
                        wsrc.rearrange("(k p) f -> p k f", p=P)[
                            :, 4 * q:4 * q + 4, b * 512:(b + 1) * 512],
                        f"cf{sl}", writes=[cfs[sl]])
                    tr.emit("DV", lambda e, _q=q, _sl=sl, _base=base:
                            e.tensor_scalar(BBpg(_base + _q), CFu_s(_sl),
                                            128.0, None, ALU.subtract),
                            reads=[cfs[sl]], writes=[bbp[base + q]])

        f2_load_block(0)
        for b in range(NBLK):
            bpar = b % 2
            wg_pgs = bbp[bpar * 8: bpar * 8 + 4]
            wu_pgs = bbp[bpar * 8 + 4: bpar * 8 + 8]
            for htl in range(4):
                if htl == 1 and b + 1 < NBLK:
                    f2_load_block(b + 1)
                if htl == 1 and b == NBLK - 1:
                    f3_load_wd(0, range(16, 32))
                ht = b * 4 + htl
                hpar = htl % 2
                g_pg, u_pg, z_pg = smp[hpar * 3], smp[hpar * 3 + 1], \
                    smp[hpar * 3 + 2]
                g_ap, u_ap, z_ap = SMpg(hpar * 3), SMpg(hpar * 3 + 1), \
                    SMpg(hpar * 3 + 2)

                def gmm(e, _base=bpar * 8, _htl=htl):
                    for k in range(NKE):
                        for n in range(4):
                            ins = e.matmul(
                                ps0[:, n * 512:(n + 1) * 512],
                                BB[:, (_base + k // 4) * 2048
                                   + (k % 4) * 512 + _htl * P:
                                   (_base + k // 4) * 2048
                                   + (k % 4) * 512 + _htl * P + P],
                                h1T(k, n * 512, 512),
                                start=(k == 0), stop=(k == NKE - 1))
                    return ins
                tr.emit("PE", gmm, reads=wg_pgs + [rAT], writes=bank[0:4])
                tr.emit("AC", lambda e, _ht=ht, _g=g_ap: e.activation(
                    _g, ps0[:], AF.Silu, bias=0.0,
                    scale=sgv_t[:, _ht:_ht + 1]),
                    reads=bank[0:4] + [st_res("sgv")], writes=[g_pg])

                def umm(e, _base=bpar * 8 + 4, _htl=htl):
                    for k in range(NKE):
                        for n in range(4):
                            ins = e.matmul(
                                ps1[:, n * 512:(n + 1) * 512],
                                BB[:, (_base + k // 4) * 2048
                                   + (k % 4) * 512 + _htl * P:
                                   (_base + k // 4) * 2048
                                   + (k % 4) * 512 + _htl * P + P],
                                h1T(k, n * 512, 512),
                                start=(k == 0), stop=(k == NKE - 1))
                    return ins
                tr.emit("PE", umm, reads=wu_pgs + [rAT], writes=bank[4:8])
                tr.emit("AC", lambda e, _ht=ht, _u=u_ap: e.activation(
                    _u, ps1[:], AF.Copy, bias=0.0,
                    scale=suv_t[:, _ht:_ht + 1]),
                    reads=bank[4:8] + [st_res("suv")], writes=[u_pg])
                tr.emit("DV", lambda e, _g=g_ap, _u=u_ap, _z=z_ap:
                        e.tensor_tensor(_z, _g, _u, ALU.mult),
                        reads=[g_pg, u_pg], writes=[z_pg])
                dma("GP",
                    zT_d[:, :, ht, :].transpose([1, 0, 2]),
                    z_ap.rearrange("p (t c) -> p t c", c=P),
                    "zTw", reads=[z_pg], writes=[rzT])


        y_half = SMf[:, 6 * 1024:7 * 1024]
        sdb_half = SMf[:, 7 * 1024:8 * 1024]
        junk_f3 = AT[:, 18432:19456]
        rjunk_f3 = Res("junkF3")
        ATf = AT.bitcast(F32)
        y_tmp = ATf[:, 9728:10752]
        rytmp = Res("ytmpF3")

        outsd = []
        for nh in range(2):
            dma("SY", sdb_half,
                sd_d[:, nh * 1024:(nh + 1) * 1024].partition_broadcast(P),
                "sdb", writes=[smp[7]])
            kts = ([kt for kt in range(NHT) if not (16 <= kt < 32)]
                   if nh == 0 else range(NHT))
            f3_load_wd(nh, kts)
            if DEBUG_DUMPS:
                for kt in range(NHT):
                    dma("GP", wdbg_d[nh, kt], wd_ap(kt), "dbg",
                        reads=[wd_res(kt)], writes=[st_res(f"wdbg{nh}_{kt}")])
            for st in range(NST):
                zsl = st % 3
                zz_ap = AT[:, zsl * 6144: zsl * 6144 + NHT * P]
                zw = [rzz[zsl]] + ([rAT] if (nh == 0 and st < 3) else [])
                dma("SY", zz_ap.rearrange("p (k c) -> p k c", c=P),
                    zT_d[st], f"zz{zsl}", reads=[rzT], writes=zw)
                if DEBUG_DUMPS and nh == 0 and st < 2:
                    dma("GP", zdbg_d[st], zz_ap, "dbg", reads=[rzz[zsl]],
                        writes=[st_res(f"zdbg{st}")])

                pb = bank[0:2] if st % 2 == 0 else bank[2:4]
                pso = ps0[:, 0:1024] if st % 2 == 0 else ps0[:, 1024:2048]

                def dmm(e, _zz=zz_ap, _pso=pso):
                    for kt in range(NHT):
                        for n in range(2):
                            ins = e.matmul(
                                _pso[:, n * 512:(n + 1) * 512],
                                _zz[:, kt * P:(kt + 1) * P],
                                wd_ap(kt)[:, n * 512:(n + 1) * 512],
                                start=(kt == 0), stop=(kt == NHT - 1))
                    return ins
                tr.emit("PE", dmm, reads=[rzz[zsl]] + rwd, writes=pb)
                a_s1 = stcol_of(f"F3a_{st}_{nh}")
                a_s2 = stcol_of(f"F3b_{st}_{nh}")
                rs1 = st_res(f"F3a_{st}_{nh}")
                rs2 = st_res(f"F3b_{st}_{nh}")
                yw0 = [rytmp] + ([rAT] if (nh == 0 and st == 0) else [])
                tr.emit("AC", lambda e, _pso=pso: e.copy(y_tmp, _pso),
                        reads=pb, writes=yw0)
                tr.emit("DV", lambda e, _a=a_s1:
                        e.scalar_tensor_tensor(
                            y_half, y_tmp, 1.0, sdb_half, ALU.mult, ALU.mult,
                            accum_out=_a),
                        reads=[rytmp, smp[7]], writes=[rs1, smp[6]])
                jw = [rs2, rjunk_f3] + ([rAT] if (nh == 0 and st == 0)
                                        else [])
                tr.emit("DV", lambda e, _a=a_s2: e.scalar_tensor_tensor(
                    junk_f3, y_half, 1.0, y_half, ALU.mult, ALU.mult,
                    accum_out=_a),
                    reads=[smp[6]], writes=jw)
                dma("GP", y_d[st * P:(st + 1) * P,
                              nh * 1024:(nh + 1) * 1024], y_half,
                    f"yw{st}", reads=[smp[6]], writes=[ry[st][nh]])

                if nh == 1:
                    ysl = st % 2
                    xsl = 2 + st % 2
                    dma("AC", CFs(ysl), y_d[st * P:(st + 1) * P, :],
                        f"cf{ysl}", reads=[ry[st][0], ry[st][1]],
                        writes=[cfs[ysl]])
                    dma("SY", CFs(xsl), x1_d[st * P:(st + 1) * P, :],
                        f"cf{xsl}", reads=[rx1[st]], writes=[cfs[xsl]])
                    a_t1 = stcol_of("F4s1"); a_t2 = stcol_of("F4s2")
                    rt1 = st_res("F4s1"); rt2 = st_res("F4s2")
                    tr.emit("DV", lambda e, _st=st: e.tensor_tensor(
                        a_t1, stcol_of(f"F3a_{_st}_0"),
                        stcol_of(f"F3a_{_st}_1"), ALU.add),
                        reads=[st_res(f"F3a_{st}_0"), st_res(f"F3a_{st}_1")],
                        writes=[rt1])
                    tr.emit("DV", lambda e, _st=st: e.tensor_tensor(
                        a_t2, stcol_of(f"F3b_{_st}_0"),
                        stcol_of(f"F3b_{_st}_1"), ALU.add),
                        reads=[st_res(f"F3b_{st}_0"), st_res(f"F3b_{st}_1")],
                        writes=[rt2])
                    a_m = stcol_of("F4m"); a_mq = stcol_of("F4mq")
                    a_va = stcol_of("F4va"); a_lv = stcol_of("F4lv")
                    a_rs = stcol_of("F4rs"); a_nm = stcol_of("F4nm")
                    rm = st_res("F4m"); rmq = st_res("F4mq")
                    rva = st_res("F4va"); rlv = st_res("F4lv")
                    rrs = st_res("F4rs"); rnm = st_res("F4nm")
                    tr.emit("DV", lambda e: e.tensor_scalar(
                        a_m, a_t1, 1.0 / E, None, ALU.mult),
                        reads=[rt1], writes=[rm])
                    tr.emit("DV", lambda e: e.scalar_tensor_tensor(
                        a_mq, a_m, -1.0, a_m, ALU.mult, ALU.mult),
                        reads=[rm], writes=[rmq])
                    tr.emit("DV", lambda e: e.scalar_tensor_tensor(
                        a_va, a_t2, 1.0 / E, a_mq, ALU.mult, ALU.add),
                        reads=[rt2, rmq], writes=[rva])
                    tr.emit("AC", lambda e: e.activation(
                        a_lv, a_va, AF.Ln, bias=eps_t, scale=1.0),
                        reads=[rva, st_res("eps")], writes=[rlv])
                    tr.emit("AC", lambda e: e.activation(
                        a_rs, a_lv, AF.Exp, bias=0.0, scale=-0.5),
                        reads=[rlv], writes=[rrs])
                    tr.emit("DV", lambda e: e.scalar_tensor_tensor(
                        a_nm, a_m, -1.0, a_rs, ALU.mult, ALU.mult),
                        reads=[rm, rrs], writes=[rnm])
                    tr.emit("DV", lambda e, _ysl=ysl: e.tensor_scalar(
                        CFs(_ysl), CFs(_ysl), a_rs, a_nm, ALU.mult, ALU.add),
                        reads=[rrs, rnm, cfs[ysl]], writes=[cfs[ysl]])
                    tr.emit("DV", lambda e, _ysl=ysl, _xsl=xsl:
                            e.tensor_tensor(CFs(_ysl), CFs(_ysl), CFs(_xsl),
                                            ALU.add),
                            reads=[cfs[ysl], cfs[xsl]], writes=[cfs[ysl]])
                    d = dma("GP", out_d[st * P:(st + 1) * P, :], CFs(ysl),
                            "outw", reads=[cfs[ysl]],
                            writes=[st_res(f"out{st}")])
                    outsd.append(d)

        tr.raw_wait("GP", outsd[-1])
        if DEBUG_SCRATCH:
            pass

        for en in tr.e.values():
            en.sem_c = E_(nc.semaphore(f"sem_{en.name}"))
        for sname, sd in tr.streams.items():
            sd[0] = E_(nc.semaphore(f"semd_{sname}"))

        with nc.Block() as block:
            @block.sync
            def _(eng):
                for t in tr.e["SY"].thunks:
                    t(eng)

            @block.gpsimd
            def _(eng):
                for t in tr.e["GP"].thunks:
                    t(eng)

            @block.vector
            def _(eng):
                for t in tr.e["DV"].thunks:
                    t(eng)

            @block.scalar
            def _(eng):
                for t in tr.e["AC"].thunks:
                    t(eng)

            @block.tensor
            def _(eng):
                for t in tr.e["PE"].thunks:
                    t(eng)

    return nc


_PROG_CACHE = {}


def _get_program():
    if "nc" not in _PROG_CACHE:
        _PROG_CACHE["nc"] = build_program()
    return _PROG_CACHE["nc"]


def kernel(L0_attn_x, L0_attn_w_qkv, L0_attn_w_out, L0_ffn_wq_gate,
           L0_ffn_s_gate, L0_ffn_wq_up, L0_ffn_s_up, L0_ffn_wq_down,
           L0_ffn_s_down):
    x = np.ascontiguousarray(np.asarray(L0_attn_x, dtype=np.float32))
    wqkvT = np.ascontiguousarray(np.asarray(L0_attn_w_qkv, np.float32).T)
    woutT = np.ascontiguousarray(np.asarray(L0_attn_w_out, np.float32).T)
    wgT = np.ascontiguousarray(np.asarray(L0_ffn_wq_gate, np.int32).T
                               .astype(np.uint8))
    wuT = np.ascontiguousarray(np.asarray(L0_ffn_wq_up, np.int32).T
                               .astype(np.uint8))
    wdT = np.ascontiguousarray(np.asarray(L0_ffn_wq_down, np.int32).T
                               .astype(np.uint8))
    sgv = np.ascontiguousarray(
        np.asarray(L0_ffn_s_gate, np.float32).reshape(NHT, P).T)
    suv = np.ascontiguousarray(
        np.asarray(L0_ffn_s_up, np.float32).reshape(NHT, P).T)
    sd = np.ascontiguousarray(
        np.asarray(L0_ffn_s_down, np.float32).reshape(1, E))

    ii = np.arange(C, dtype=np.int64)[:, None]
    jj = np.arange(2 * C, dtype=np.int64)[None, :]
    allowed = (jj >= ii) & (jj <= ii + C)
    maskq = np.where(allowed, np.float32(0.0),
                     np.float32(MASKVAL)).astype(np.float32)
    ident = np.eye(P, dtype=np.float32)

    nc = _get_program()
    shared = dict(wqkvT=wqkvT, woutT=woutT, wgT=wgT, wuT=wuT, wdT=wdT,
                  sgv=sgv, suv=suv, sd=sd, maskq=maskq, ident=ident)
    in_maps = [dict(x0=np.ascontiguousarray(x[c]), **shared)
               for c in range(8)]
    res = run_bass_kernel_spmd(nc, in_maps, core_ids=list(range(8)))
    return np.stack([r["out"] for r in res.results], axis=0)
